# version 04
# baseline (speedup 1.0000x reference)
import math
import numpy as np

import concourse.bass as bass
import concourse.mybir as mybir
from concourse import bass_utils

F32 = mybir.dt.float32
BF16 = mybir.dt.bfloat16
AF = mybir.ActivationFunctionType
OP = mybir.AluOpType

D, S, B, H, HD, FF, NE = 2048, 4096, 4, 16, 128, 8192, 2
QT, OWN, NDC = 512, 2048, 16
SLOT_SETS = [[7, 4, 3, 0], [6, 5, 2, 1]]
NREAL = [[32, 20, 16, 4], [28, 24, 12, 8]]
NMAX = [32, 24, 16, 8]
SCALE = 1.0 / math.sqrt(2048.0)
EXPB = -4.0
EPS = 1e-5
QG, NQG = 1024, 2
FGN = 4


class Ctx:
    ENG = {"g": "gpsimd", "v": "vector", "t": "tensor", "a": "scalar", "y": "sync"}

    def __init__(self, nc, block):
        self.nc, self.block = nc, block
        self.q = {e: [] for e in self.ENG}
        self.sem, self.cnt = {}, {}
        self.guard = {}
        self.sem_cms = []

    def sema(self, name):
        cm = self.nc.semaphore(name)
        self.sem[name] = cm.__enter__()
        self.sem_cms.append(cm)
        self.cnt[name] = 0

    def emit(self, e, fn, inc=None, incby=1):
        h = None
        if inc is not None:
            self.cnt[inc] += incby
            h = self.sem[inc]
        drained = inc is not None and e in ("v", "a")

        def f(eng, fn=fn, h=h, incby=incby, drained=drained):
            ins = fn(eng)
            if h is not None:
                if drained:
                    eng.drain().then_inc(h, incby)
                else:
                    ins.then_inc(h, incby)
        self.q[e].append(f)
        return self.cnt[inc] if inc is not None else None

    def wait(self, e, name, val):
        if not val or val <= 0:
            return
        h = self.sem[name]
        self.q[e].append(lambda eng, h=h, val=val: eng.wait_ge(h, val))

    def wait_banks(self, e, banks):
        need = {}
        for b in banks:
            if b in self.guard:
                sn, v = self.guard[b]
                need[sn] = max(need.get(sn, 0), v)
        for sn, v in need.items():
            self.wait(e, sn, v)

    def set_guard(self, banks, semname, val):
        for b in banks:
            self.guard[b] = (semname, val)

    def barrier(self):
        for e in self.ENG:
            for sn in self.sem:
                self.wait(e, sn, self.cnt[sn])

    def flush(self):
        for e, lst in self.q.items():
            if not lst:
                continue

            def body(eng, lst=tuple(lst)):
                for f in lst:
                    f(eng)
            getattr(self.block, self.ENG[e])(body)
        self.q = {e: [] for e in self.ENG}


def build_nc(debug=False):
    nc = bass.Bass("TRN2", target_bir_lowering=False)

    d_xT = nc.dram_tensor("xT", [D, S], F32, kind="ExternalInput").ap()
    d_xo = nc.dram_tensor("xown", [D, OWN], F32, kind="ExternalInput").ap()
    d_thr = nc.dram_tensor("thr", [128, 128], F32, kind="ExternalInput").ap()
    d_iota = nc.dram_tensor("iota", [128, QT], F32, kind="ExternalInput").ap()
    d_wqT = nc.dram_tensor("wqT", [D, D], F32, kind="ExternalInput").ap()
    d_wkT = nc.dram_tensor("wkT", [D, D], F32, kind="ExternalInput").ap()
    d_wvT = nc.dram_tensor("wvT", [D, D], F32, kind="ExternalInput").ap()
    d_wpT = nc.dram_tensor("wpT", [D, D], F32, kind="ExternalInput").ap()
    d_gwT = nc.dram_tensor("gwT", [D, NE], F32, kind="ExternalInput").ap()
    d_wgT = nc.dram_tensor("wgT", [NE, D, FF], F32, kind="ExternalInput").ap()
    d_wuT = nc.dram_tensor("wuT", [NE, D, FF], F32, kind="ExternalInput").ap()
    d_wdT = nc.dram_tensor("wdT", [NE, FF, D], F32, kind="ExternalInput").ap()
    d_out = nc.dram_tensor("out", [D, OWN], F32, kind="ExternalOutput").ap()
    scr_kind = "ExternalOutput" if debug else "Internal"
    d_kscr = nc.dram_tensor("kscr", [D, S], BF16, kind=scr_kind).ap()
    d_vscr = nc.dram_tensor("vscr", [H, S, HD], BF16, kind=scr_kind).ap()
    d_x1scr = nc.dram_tensor("x1scr", [D, OWN], BF16, kind=scr_kind).ap()
    d_dbq = nc.dram_tensor("dbq", [D, OWN], BF16, kind="ExternalOutput").ap() if debug else None
    d_dbo = nc.dram_tensor("dbo", [D, OWN], BF16, kind="ExternalOutput").ap() if debug else None
    d_dba = nc.dram_tensor("dba", [D, QT], F32, kind="ExternalOutput").ap() if debug else None
    d_dbx = nc.dram_tensor("dbx", [128, QT], F32, kind="ExternalOutput").ap() if debug else None
    d_dby = nc.dram_tensor("dby", [D, QG], F32, kind="ExternalOutput").ap() if debug else None
    d_dbc = nc.dram_tensor("dbc", [NE, 128, QG], F32, kind="ExternalOutput").ap() if debug else None
    d_dbb = nc.dram_tensor("dbb", [2, 128, QT], F32, kind="ExternalOutput").ap() if debug else None

    open_cms = []

    def alloc(name, shape, dtype, side="left", psum=False):
        cm = nc.psum_tensor(name, shape, dtype) if psum else nc.sbuf_tensor(name, shape, dtype, side=side)
        h = cm.__enter__()
        open_cms.append(cm)
        return h, cm

    def free(*cmlist):
        for cm in cmlist:
            cm.__exit__(None, None, None)
            open_cms.remove(cm)

    from concourse.hw_specs import get_activation_tables
    act_sets = list(get_activation_tables(nc.m.arch))
    LNEXP_ID = act_sets.index("natural_log_exp_and_others")
    SILU_ID = act_sets.index("silu_and_others")

    iota_sb, _ = alloc("iota_sb", [128, QT], F32)
    thr_sb, _ = alloc("thr_sb", [128, 128], F32)
    onesk, _ = alloc("onesk", [128, 1], BF16)
    ones1, _ = alloc("ones1", [1, 128], BF16)
    c_scale, _ = alloc("c_scale", [128, 1], F32)
    c_expb, _ = alloc("c_expb", [128, 1], F32)
    c_onef, _ = alloc("c_onef", [128, 1], F32)
    c_eps, _ = alloc("c_eps", [128, 1], F32)
    c_mhalf, _ = alloc("c_mhalf", [128, 1], F32)
    c_mone, _ = alloc("c_mone", [128, 1], F32)
    wst = [alloc(f"wst{i}", [128, NDC * 128], BF16)[0] for i in range(2)]
    row_a, _ = alloc("row_a", [1, QT], F32)
    row_b, _ = alloc("row_b", [1, QT], F32)
    row_c, _ = alloc("row_c", [1, QT], F32)
    row_d, _ = alloc("row_d", [1, QT], F32)
    row_bf, _ = alloc("row_bf", [1, QT], BF16)
    row_bf2, _ = alloc("row_bf2", [1, QT], BF16)
    bb0, _ = alloc("bb0", [128, QT], F32)
    bb1, _ = alloc("bb1", [128, QT], F32)
    stb = [alloc(f"stb{i}", [128, QT], BF16)[0] for i in range(2)]
    sqb = [alloc(f"sqb{i}", [128, QT], BF16)[0] for i in range(2)]

    pA, _ = alloc("pA", [128, 2048], F32, psum=True)
    pB, _ = alloc("pB", [128, 2048], F32, psum=True)

    with nc.Block() as block:
        ctx = Ctx(nc, block)
        for sname in ("sT", "sV", "sA", "sInit", "sXb", "sWvh", "sDbg",
                      "sWst0", "sWst1", "sKst0", "sKst1", "sVstg0", "sVstg1",
                      "sKvh0", "sKvh1", "sXres0", "sXres1", "sX1st0", "sX1st1",
                      "sOut0", "sOut1", "sWdt0", "sWdt1",
                      "sWgu0", "sWgu1", "sWgu2", "sWgu3"):
            ctx.sema(sname)
        E, W, WB, SG = ctx.emit, ctx.wait, ctx.wait_banks, ctx.set_guard

        def preload(set_id):
            def f(a, set_id=set_id):
                return a.add_instruction(mybir.InstLoadActFuncSet(
                    name=nc.get_next_instruction_name(), act_func_set_id=set_id,
                    ins=[], outs=[]))
            E("a", f)

        for buf, val in ((onesk, 1.0), (ones1, 1.0), (c_scale, SCALE), (c_expb, EXPB),
                         (c_onef, 1.0), (c_eps, EPS), (c_mhalf, -0.5), (c_mone, -1.0)):
            E("g", lambda g, b=buf, v=val: g.memset(b[:, :], v))
        E("g", lambda g: g.dma_start(out=iota_sb[:, :], in_=d_iota), inc="sInit", incby=16)
        init_done = E("g", lambda g: g.dma_start(out=thr_sb[:, :], in_=d_thr), inc="sInit", incby=16)

        wst_use = [0, 0]

        def load_wst(dram, col, sl):
            W("g", "sT", wst_use[sl])
            def ld(g, dram=dram, col=col, sl=sl):
                src = dram[:, col * 128:(col + 1) * 128].rearrange("(c p) m -> p c m", p=128)
                dst = wst[sl][:, :].rearrange("p (c m) -> p c m", c=NDC)
                return g.dma_start(out=dst, in_=src)
            return E("g", ld, inc="sWst%d" % sl, incby=16)

        kst = [alloc(f"kst{i}", [128, 2048], BF16)[0] for i in range(2)]
        kst_cms = open_cms[-2:]
        vstg = [alloc(f"vstg{i}", [128, 1024], BF16)[0] for i in range(2)]
        vstg_cms = open_cms[-2:]
        wvh, wvh_cm = alloc("wvh", [128, NDC * 1024], BF16)
        xb, xb_cm = alloc("xb", [128, NDC * 2048], BF16, side="right")

        kst_free = [0, 0]
        vstg_free = [0, 0]

        for tg in range(2):
            W("g", "sT", ctx.cnt["sT"])
            xb_ld = 0
            for i in range(4):
                def ld_xb(g, i=i, tg=tg):
                    src = d_xT[i * 512:(i + 1) * 512, tg * 2048:(tg + 1) * 2048]
                    src = src.rearrange("(c p) t -> p c t", p=128)
                    dst = xb[:, i * 8192:(i + 1) * 8192].rearrange("p (c t) -> p c t", c=4)
                    return g.dma_start(out=dst, in_=src)
                xb_ld = E("g", ld_xb, inc="sXb", incby=16)

            for ks in range(NDC):
                sl = ks % 2
                wld = load_wst(d_wkT, ks, sl)
                ps = pA if sl == 0 else pB
                banks = [("A" if sl == 0 else "B") + str(i) for i in range(4)]
                W("t", "sWst%d" % sl, wld)
                W("t", "sXb", xb_ld)
                W("t", "sInit", init_done)
                WB("t", banks)
                for dc in range(NDC):
                    for nt in range(4):
                        def mm(t, ps=ps, sl=sl, dc=dc, nt=nt):
                            return t.matmul(ps[:, nt * 512:(nt + 1) * 512],
                                            wst[sl][:, dc * 128:(dc + 1) * 128],
                                            xb[:, dc * 2048 + nt * 512:dc * 2048 + (nt + 1) * 512],
                                            start=(dc == 0), stop=(dc == NDC - 1))
                        E("t", mm, inc=("sT" if dc == NDC - 1 and nt == 3 else None))
                wst_use[sl] = ctx.cnt["sT"]
                W("v", "sT", ctx.cnt["sT"])
                W("v", "sKst%d" % sl, kst_free[sl])
                ev = E("v", lambda v, ps=ps, sl=sl: v.tensor_copy(kst[sl][:, :], ps[:, :]), inc="sV")
                SG(banks, "sV", ev)
                W("y", "sV", ev)
                def stk(y, ks=ks, sl=sl, tg=tg):
                    return y.dma_start(out=d_kscr[ks * 128:(ks + 1) * 128, tg * 2048:(tg + 1) * 2048],
                                       in_=kst[sl][:, :])
                kst_free[sl] = E("y", stk, inc="sKst%d" % sl, incby=16)

            for vh2 in range(2):
                W("g", "sT", ctx.cnt["sT"])
                def ld_wv(g, vh2=vh2):
                    src = d_wvT[:, vh2 * 1024:(vh2 + 1) * 1024]
                    src = src.rearrange("(c p) m -> p c m", p=128)
                    dst = wvh[:, :].rearrange("p (c m) -> p c m", c=NDC)
                    return g.dma_start(out=dst, in_=src)
                wv_ld = E("g", ld_wv, inc="sWvh", incby=16)
                for tb in range(16):
                    sl = tb % 2
                    ps = pA if sl == 0 else pB
                    banks = [("A" if sl == 0 else "B") + str(i) for i in range(2)]
                    W("t", "sWvh", wv_ld)
                    WB("t", banks)
                    for dc in range(NDC):
                        for nt in range(2):
                            def mmv(t, ps=ps, dc=dc, nt=nt, tb=tb):
                                return t.matmul(ps[:, nt * 512:(nt + 1) * 512],
                                                xb[:, dc * 2048 + tb * 128:dc * 2048 + (tb + 1) * 128],
                                                wvh[:, dc * 1024 + nt * 512:dc * 1024 + (nt + 1) * 512],
                                                start=(dc == 0), stop=(dc == NDC - 1))
                            E("t", mmv, inc=("sT" if dc == NDC - 1 and nt == 1 else None))
                    W("v", "sT", ctx.cnt["sT"])
                    W("v", "sVstg%d" % sl, vstg_free[sl])
                    ev = E("v", lambda v, ps=ps, sl=sl: v.tensor_copy(vstg[sl][:, :], ps[:, 0:1024]),
                           inc="sV")
                    SG(banks, "sV", ev)
                    W("y", "sV", ev)
                    def stv(y, vh2=vh2, tg=tg, tb=tb, sl=sl):
                        dst = d_vscr[vh2 * 8:(vh2 + 1) * 8,
                                     tg * 2048 + tb * 128:tg * 2048 + (tb + 1) * 128, :]
                        dst = dst.rearrange("h t v -> t h v")
                        srcv = vstg[sl][:, :].rearrange("t (h v) -> t h v", h=8)
                        return y.dma_start(out=dst, in_=srcv)
                    vstg_free[sl] = E("y", stv, inc="sVstg%d" % sl, incby=16)
        kv_store_total = {n: ctx.cnt[n] for n in ("sKst0", "sKst1", "sVstg0", "sVstg1")}
        ctx.flush()
        free(wvh_cm, *reversed(vstg_cms), *reversed(kst_cms))

        qsb, qsb_cm = alloc("qsb", [128, H * 2048], BF16)
        W("g", "sT", ctx.cnt["sT"])
        xb_ld = 0
        for i in range(4):
            def ld_xo(g, i=i):
                src = d_xo[i * 512:(i + 1) * 512, :].rearrange("(c p) t -> p c t", p=128)
                dst = xb[:, i * 8192:(i + 1) * 8192].rearrange("p (c t) -> p c t", c=4)
                return g.dma_start(out=dst, in_=src)
            xb_ld = E("g", ld_xo, inc="sXb", incby=16)
        for qs in range(NDC):
            sl = qs % 2
            wld = load_wst(d_wqT, qs, sl)
            ps = pA if sl == 0 else pB
            banks = [("A" if sl == 0 else "B") + str(i) for i in range(4)]
            W("t", "sWst%d" % sl, wld)
            W("t", "sXb", xb_ld)
            WB("t", banks)
            for dc in range(NDC):
                for nt in range(4):
                    def mmq(t, ps=ps, sl=sl, dc=dc, nt=nt):
                        return t.matmul(ps[:, nt * 512:(nt + 1) * 512],
                                        wst[sl][:, dc * 128:(dc + 1) * 128],
                                        xb[:, dc * 2048 + nt * 512:dc * 2048 + (nt + 1) * 512],
                                        start=(dc == 0), stop=(dc == NDC - 1))
                    E("t", mmq, inc=("sT" if dc == NDC - 1 and nt == 3 else None))
            wst_use[sl] = ctx.cnt["sT"]
            W("v", "sT", ctx.cnt["sT"])
            if qs == 0:
                for n_, v_ in kv_store_total.items():
                    W("v", n_, v_)
            ev = E("v", lambda v, ps=ps, qs=qs: v.tensor_copy(qsb[:, qs * 2048:(qs + 1) * 2048],
                                                              ps[:, :]), inc="sV")
            SG(banks, "sV", ev)
        q_done = ctx.cnt["sV"]
        if debug:
            W("y", "sV", q_done)
            for qs in range(NDC):
                E("y", lambda y, qs=qs: y.dma_start(
                    out=d_dbq[qs * 128:(qs + 1) * 128, :],
                    in_=qsb[:, qs * 2048:(qs + 1) * 2048]), inc="sDbg", incby=16)
        ctx.flush()
        free(xb_cm)

        khb = [alloc(f"khb{i}", [128, S], BF16)[0] for i in range(2)]
        khb_cms = open_cms[-2:]
        vhb = [alloc(f"vhb{i}", [128, S], BF16)[0] for i in range(2)]
        vhb_cms = open_cms[-2:]
        ptile = [alloc(f"ptile{i}", [128, QT], BF16)[0] for i in range(4)]
        pt_cms = open_cms[-4:]
        pmtile = [alloc(f"pmtile{i}", [128, QT], BF16)[0] for i in range(4)]
        pm_cms = open_cms[-4:]
        osb, osb_cm = alloc("osb", [128, H * 2048], BF16, side="right")

        preload(LNEXP_ID)

        head_free = [0, 0]
        kvh_ld = [0, 0]
        last_bcast = 0
        last_onorm = 0
        for h in range(H):
            sl = h % 2
            for n_, v_ in kv_store_total.items():
                W("y", n_, v_)
            W("y", "sT", head_free[sl])
            E("y", lambda y, h=h, sl=sl: y.dma_start(out=khb[sl][:, :],
                                                     in_=d_kscr[h * 128:(h + 1) * 128, :]),
              inc="sKvh%d" % sl, incby=16)
            def ldv(y, h=h, sl=sl):
                src = d_vscr[h].rearrange("(kb p) v -> p kb v", p=128)
                dst = vhb[sl][:, :].rearrange("p (kb v) -> p kb v", kb=32)
                return y.dma_start(out=dst, in_=src)
            kvh_ld[sl] = E("y", ldv, inc="sKvh%d" % sl, incby=16)

            W("t", "sKvh%d" % sl, kvh_ld[sl])
            if h == 0:
                W("t", "sV", q_done)
            for j in range(4):
                n = NMAX[j]
                jp = j % 2
                o_ps = pA[:, jp * 512:(jp + 1) * 512]
                o_banks = ["A%d" % jp]
                sum_ps = pA[0:1, 1024 + jp * 512:1024 + (jp + 1) * 512]
                sum_banks = ["A%d" % (2 + jp)]
                exp_cnt = [0] * n
                mask_cnt = [0] * n
                pv_cnt = [0] * n

                def emit_pv(kb):
                    W("t", "sV", mask_cnt[kb])
                    def mmPV(t, h=h, kb=kb, n=n, o_ps=o_ps):
                        return t.matmul(o_ps, vhb[h % 2][:, kb * 128:(kb + 1) * 128],
                                        pmtile[kb % 4][:, :], start=(kb == 0), stop=(kb == n - 1))
                    E("t", mmPV)
                    def mmSU(t, kb=kb, n=n, sum_ps=sum_ps):
                        return t.matmul(sum_ps, onesk[:, 0:1], pmtile[kb % 4][:, :],
                                        start=(kb == 0), stop=(kb == n - 1))
                    pv_cnt[kb] = E("t", mmSU, inc="sT")

                for kb in range(n):
                    ksl = kb % 4
                    s_ps = pB[:, ksl * 512:(ksl + 1) * 512]
                    s_banks = ["B%d" % ksl]
                    WB("t", s_banks)
                    if kb == 0:
                        WB("t", o_banks)
                        WB("t", sum_banks)
                    def mmS(t, h=h, j=j, kb=kb, s_ps=s_ps):
                        return t.matmul(s_ps, khb[h % 2][:, kb * 128:(kb + 1) * 128],
                                        qsb[:, h * 2048 + j * 512:h * 2048 + (j + 1) * 512],
                                        start=True, stop=True)
                    s_cnt = E("t", mmS, inc="sT")
                    W("a", "sT", s_cnt)
                    if kb >= 4:
                        W("a", "sV", mask_cnt[kb - 4])
                    def act_exp(a, s_ps=s_ps, ksl=ksl):
                        return a.activation(ptile[ksl][:, :], s_ps, AF.Exp,
                                            bias=c_expb[:, 0:1], scale=c_scale[:, 0:1])
                    exp_cnt[kb] = E("a", act_exp, inc="sA")
                    SG(s_banks, "sA", exp_cnt[kb])
                    W("v", "sA", exp_cnt[kb])
                    if kb >= 4:
                        W("v", "sT", pv_cnt[kb - 4])
                    col = j * 32 + kb
                    def msk(v, col=col, ksl=ksl):
                        return v.scalar_tensor_tensor(pmtile[ksl][:, :], iota_sb[:, :],
                                                      thr_sb[:, col:col + 1], ptile[ksl][:, :],
                                                      OP.is_ge, OP.mult)
                    mask_cnt[kb] = E("v", msk, inc="sV")
                    if kb >= 2:
                        emit_pv(kb - 2)
                emit_pv(n - 2)
                emit_pv(n - 1)

                W("a", "sT", pv_cnt[n - 1])
                W("a", "sT", last_bcast)
                E("a", lambda a, sum_ps=sum_ps: a.activation(row_a[:, :], sum_ps, AF.Ln))
                E("a", lambda a: a.activation(row_b[:, :], row_a[:, :], AF.Exp,
                                              scale=c_mone[0:1, 0:1]))
                rb_done = E("a", lambda a: a.copy(row_bf[:, :], row_b[:, :]), inc="sA")
                SG(sum_banks, "sA", rb_done)
                bc_ps = pB[:, 1024 + jp * 512:1024 + (jp + 1) * 512]
                bc_banks = ["B%d" % (2 + jp)]
                W("t", "sA", rb_done)
                WB("t", bc_banks)
                last_bcast = E("t", lambda t, bc_ps=bc_ps: t.matmul(
                    bc_ps, ones1[0:1, :], row_bf[0:1, :], start=True, stop=True), inc="sT")
                W("a", "sT", last_bcast)
                W("a", "sV", last_onorm)
                rbb = E("a", lambda a, bc_ps=bc_ps: a.copy(bb0[:, :], bc_ps), inc="sA")
                SG(bc_banks, "sA", rbb)
                W("v", "sA", rbb)
                def onorm(v, h=h, j=j, o_ps=o_ps):
                    return v.scalar_tensor_tensor(
                        osb[:, h * 2048 + j * 512:h * 2048 + (j + 1) * 512],
                        o_ps, c_onef[:, 0:1], bb0[:, :], OP.mult, OP.mult)
                last_onorm = E("v", onorm, inc="sV")
                SG(o_banks, "sV", last_onorm)
            head_free[sl] = ctx.cnt["sT"]
        o_done = ctx.cnt["sV"]
        if debug:
            W("y", "sV", o_done)
            for qs in range(NDC):
                E("y", lambda y, qs=qs: y.dma_start(
                    out=d_dbo[qs * 128:(qs + 1) * 128, :],
                    in_=osb[:, qs * 2048:(qs + 1) * 2048]), inc="sDbg", incby=16)
        ctx.flush()
        free(*reversed(pm_cms), *reversed(pt_cms), *reversed(vhb_cms), *reversed(khb_cms), qsb_cm)

        a1b, a1b_cm = alloc("a1b", [128, NDC * 512], F32)
        xres = [alloc(f"xres{i}", [128, QT], F32)[0] for i in range(2)]
        xres_cms = open_cms[-2:]
        x1st = [alloc(f"x1st{i}", [128, QT], BF16)[0] for i in range(2)]
        x1st_cms = open_cms[-2:]
        ttb, ttb_cm = alloc("ttb", [128, QT], F32)

        xres_free = [0, 0]
        x1_free = [0, 0]
        prev_stats_sA = 0

        def ln_rows_and_bcast():
            E("v", lambda v: v.tensor_scalar(row_a[:, :], pA[0:1, 0:512], 1.0 / D, None, OP.mult))
            E("v", lambda v: v.tensor_mul(row_b[:, :], row_a[:, :], row_a[:, :]))
            vd = E("v", lambda v: v.scalar_tensor_tensor(row_c[:, :], pA[0:1, 512:1024],
                                                         1.0 / D, row_b[:, :],
                                                         OP.mult, OP.subtract), inc="sV")
            SG(["A0", "A1"], "sV", vd)
            W("a", "sV", vd)
            E("a", lambda a: a.activation(row_c[:, :], row_c[:, :], AF.Ln, bias=c_eps[0:1, 0:1]))
            rsd = E("a", lambda a: a.activation(row_c[:, :], row_c[:, :], AF.Exp,
                                                scale=c_mhalf[0:1, 0:1]), inc="sA")
            W("v", "sA", rsd)
            E("v", lambda v: v.tensor_mul(row_d[:, :], row_a[:, :], row_c[:, :]))
            E("v", lambda v: v.tensor_copy(row_bf[:, :], row_c[:, :]))
            mrd = E("v", lambda v: v.tensor_copy(row_bf2[:, :], row_d[:, :]), inc="sV")
            W("t", "sV", mrd)
            WB("t", ["A2", "A3"])
            E("t", lambda t: t.matmul(pA[:, 1024:1536], ones1[0:1, :], row_bf[0:1, :],
                                      start=True, stop=True))
            b2 = E("t", lambda t: t.matmul(pA[:, 1536:2048], ones1[0:1, :], row_bf2[0:1, :],
                                           start=True, stop=True), inc="sT")
            W("a", "sT", b2)
            E("a", lambda a: a.copy(bb0[:, :], pA[:, 1024:1536]))
            cpd = E("a", lambda a: a.copy(bb1[:, :], pA[:, 1536:2048]), inc="sA")
            SG(["A2", "A3"], "sA", cpd)
            return cpd

        W("y", "sT", ctx.cnt["sT"])
        W("y", "sV", o_done)
        for qt in range(4):
            for ds in range(NDC):
                sl = ds % 2
                wld = load_wst(d_wpT, ds, sl)
                W("y", "sV", xres_free[sl])
                xr = E("y", lambda y, ds=ds, qt=qt, sl=sl: y.dma_start(
                    out=xres[sl][:, :],
                    in_=d_xo[ds * 128:(ds + 1) * 128, qt * 512:(qt + 1) * 512]),
                    inc="sXres%d" % sl, incby=16)
                ps = pB[:, sl * 512:(sl + 1) * 512]
                banks = ["B%d" % sl]
                W("t", "sWst%d" % sl, wld)
                if qt == 0 and ds == 0:
                    W("t", "sV", o_done)
                WB("t", banks)
                for dc in range(NDC):
                    def mmp(t, ps=ps, sl=sl, dc=dc, qt=qt):
                        return t.matmul(ps, wst[sl][:, dc * 128:(dc + 1) * 128],
                                        osb[:, dc * 2048 + qt * 512:dc * 2048 + (qt + 1) * 512],
                                        start=(dc == 0), stop=(dc == NDC - 1))
                    E("t", mmp, inc=("sT" if dc == NDC - 1 else None))
                wst_use[sl] = ctx.cnt["sT"]
                W("v", "sT", ctx.cnt["sT"])
                W("v", "sXres%d" % sl, xr)
                W("v", "sA", prev_stats_sA)
                ad = E("v", lambda v, ps=ps, ds=ds, sl=sl: v.tensor_add(
                    a1b[:, ds * 512:(ds + 1) * 512], ps, xres[sl][:, :]), inc="sV")
                xres_free[sl] = ad
                SG(banks, "sV", ad)
                if debug and qt == 0 and ds == 0:
                    W("y", "sV", ad)
                    E("y", lambda y: y.dma_start(out=d_dbx, in_=xres[0][:, :]),
                      inc="sDbg", incby=16)
                    W("y", "sDbg", ctx.cnt["sDbg"])
            a1_done = ctx.cnt["sV"]

            W("a", "sV", a1_done)
            st_cnt = [0] * NDC
            for dsc in range(NDC):
                sl = dsc % 2
                if dsc >= 2:
                    W("a", "sT", st_cnt[dsc - 2])
                E("a", lambda a, dsc=dsc, sl=sl: a.copy(stb[sl][:, :],
                                                        a1b[:, dsc * 512:(dsc + 1) * 512]))
                sq = E("a", lambda a, dsc=dsc, sl=sl: a.activation(
                    sqb[sl][:, :], a1b[:, dsc * 512:(dsc + 1) * 512], AF.Square), inc="sA")
                W("t", "sA", sq)
                if dsc == 0:
                    WB("t", ["A0", "A1"])
                E("t", lambda t, sl=sl, dsc=dsc: t.matmul(pA[0:1, 0:512], onesk[:, 0:1],
                                                          stb[sl][:, :], start=(dsc == 0),
                                                          stop=(dsc == NDC - 1)))
                st_cnt[dsc] = E("t", lambda t, sl=sl, dsc=dsc: t.matmul(
                    pA[0:1, 512:1024], onesk[:, 0:1], sqb[sl][:, :],
                    start=(dsc == 0), stop=(dsc == NDC - 1)), inc="sT")
            prev_stats_sA = ctx.cnt["sA"]
            if debug and qt == 0:
                W("y", "sV", a1_done)
                for dsc in range(NDC):
                    E("y", lambda y, dsc=dsc: y.dma_start(
                        out=d_dba[dsc * 128:(dsc + 1) * 128, :],
                        in_=a1b[:, dsc * 512:(dsc + 1) * 512]), inc="sDbg", incby=16)
                W("v", "sDbg", ctx.cnt["sDbg"])
                W("a", "sDbg", ctx.cnt["sDbg"])
            W("v", "sT", st_cnt[NDC - 1])
            bb_done = ln_rows_and_bcast()
            W("v", "sA", bb_done)
            if debug and qt == 0:
                W("y", "sA", bb_done)
                E("y", lambda y: y.dma_start(out=d_dbb[0], in_=bb0[:, :]), inc="sDbg", incby=16)
                E("y", lambda y: y.dma_start(out=d_dbb[1], in_=bb1[:, :]), inc="sDbg", incby=16)
                W("v", "sDbg", ctx.cnt["sDbg"])
                W("a", "sDbg", ctx.cnt["sDbg"])
            for dsc in range(NDC):
                sl = dsc % 2
                W("v", "sX1st%d" % sl, x1_free[sl])
                E("v", lambda v, dsc=dsc: v.tensor_mul(ttb[:, :],
                                                       a1b[:, dsc * 512:(dsc + 1) * 512], bb0[:, :]))
                nv = E("v", lambda v, sl=sl: v.tensor_sub(x1st[sl][:, :], ttb[:, :], bb1[:, :]),
                       inc="sV")
                W("y", "sV", nv)
                x1_free[sl] = E("y", lambda y, dsc=dsc, qt=qt, sl=sl: y.dma_start(
                    out=d_x1scr[dsc * 128:(dsc + 1) * 128, qt * 512:(qt + 1) * 512],
                    in_=x1st[sl][:, :]), inc="sX1st%d" % sl, incby=16)
        x1_total = {n: ctx.cnt[n] for n in ("sX1st0", "sX1st1")}
        ctx.flush()
        free(ttb_cm, *reversed(x1st_cms), *reversed(xres_cms), a1b_cm, osb_cm)

        x1qg, _ = alloc("x1qg", [128, NDC * QG], BF16)
        yacc, _ = alloc("yacc", [128, NDC * QG], F32)
        hsb, _ = alloc("hsb", [128, NDC * QG], BF16)
        wgu = [alloc(f"wgu{i}", [128, NDC * 128], BF16)[0] for i in range(4)]
        wdt = [alloc(f"wdt{i}", [128, NDC * 128], BF16)[0] for i in range(2)]
        cbb = [alloc(f"cbb{i}", [128, QG], F32)[0] for i in range(2)]
        sgt = [alloc(f"sgt{i}", [128, QG], BF16)[0] for i in range(2)]
        ytmp = [alloc(f"ytmp{i}", [128, QG], F32)[0] for i in range(2)]
        outst = [alloc(f"outst{i}", [128, QT], F32)[0] for i in range(2)]
        ttb2, _ = alloc("ttb2", [128, QT], F32)
        gwb, _ = alloc("gwb", [128, NDC * NE], BF16)

        ctx.barrier()
        def ld_gw(g):
            src = d_gwT.rearrange("(c p) m -> p c m", p=128)
            dst = gwb[:, :].rearrange("p (c m) -> p c m", c=NDC)
            return g.dma_start(out=dst, in_=src)
        gw_ld = E("g", ld_gw, inc="sInit", incby=16)

        wgu_use = [0, 0, 0, 0]
        wdt_use = [0, 0]
        outst_free = [0, 0]

        for qg in range(NQG):
            if qg > 0:
                ctx.barrier()
            x1g_ld = 0
            for n_, v_ in x1_total.items():
                W("g", n_, v_)
            for i in range(4):
                def ldx1(g, i=i, qg=qg):
                    src = d_x1scr[i * 512:(i + 1) * 512, qg * QG:(qg + 1) * QG]
                    src = src.rearrange("(c p) t -> p c t", p=128)
                    dst = x1qg[:, i * 4 * QG:(i + 1) * 4 * QG].rearrange("p (c t) -> p c t", c=4)
                    return g.dma_start(out=dst, in_=src)
                x1g_ld = E("g", ldx1, inc="sXb", incby=16)

            preload(SILU_ID)
            W("t", "sInit", gw_ld)
            W("t", "sXb", x1g_ld)
            WB("t", ["A0", "A1", "A2", "A3"])
            glog = 0
            for e in range(NE):
                for dc in range(NDC):
                    for nt in range(2):
                        def mmg(t, e=e, dc=dc, nt=nt):
                            return t.matmul(pA[0:1, e * 1024 + nt * 512:e * 1024 + (nt + 1) * 512],
                                            gwb[:, dc * NE + e:dc * NE + e + 1],
                                            x1qg[:, dc * QG + nt * 512:dc * QG + (nt + 1) * 512],
                                            start=(dc == 0), stop=(dc == NDC - 1))
                        r = E("t", mmg, inc=("sT" if dc == NDC - 1 and nt == 1 else None))
                        if r is not None:
                            glog = r
            for hq in range(2):
                W("a", "sT", glog)
                l1c = E("a", lambda a, hq=hq: a.copy(
                    row_d[:, :], pA[0:1, 1024 + hq * 512:1024 + (hq + 1) * 512]), inc="sA")
                SG(["A%d" % (2 + hq)], "sA", l1c)
                W("v", "sA", l1c)
                df = E("v", lambda v, hq=hq: v.tensor_sub(
                    row_a[:, :], pA[0:1, hq * 512:(hq + 1) * 512], row_d[:, :]), inc="sV")
                SG(["A%d" % hq], "sV", df)
                W("a", "sV", df)
                tn = E("a", lambda a: a.activation(row_b[:, :], row_a[:, :], AF.Tanh,
                                                   scale=c_mhalf[0:1, 0:1]), inc="sA")
                W("v", "sA", tn)
                E("v", lambda v: v.tensor_scalar(row_b[:, :], row_b[:, :], -0.5, 0.5,
                                                 OP.mult, OP.add))
                E("v", lambda v: v.scalar_tensor_tensor(row_c[:, :], row_a[:, :], 0.0,
                                                        row_b[:, :], OP.is_ge, OP.mult))
                E("v", lambda v: v.tensor_scalar(row_b[:, :], row_b[:, :], -1.0, 1.0,
                                                 OP.mult, OP.add))
                cr = E("v", lambda v: v.scalar_tensor_tensor(row_d[:, :], row_a[:, :], 0.0,
                                                             row_b[:, :], OP.is_lt, OP.mult),
                       inc="sV")
                W("a", "sV", cr)
                E("a", lambda a: a.copy(row_bf[:, :], row_c[:, :]))
                cpc = E("a", lambda a: a.copy(row_bf2[:, :], row_d[:, :]), inc="sA")
                W("t", "sA", cpc)
                WB("t", ["B2", "B3"])
                E("t", lambda t: t.matmul(pB[:, 1024:1536], ones1[0:1, :], row_bf[0:1, :],
                                          start=True, stop=True))
                bc = E("t", lambda t: t.matmul(pB[:, 1536:2048], ones1[0:1, :], row_bf2[0:1, :],
                                               start=True, stop=True), inc="sT")
                W("a", "sT", bc)
                E("a", lambda a, hq=hq: a.copy(cbb[0][:, hq * 512:(hq + 1) * 512], pB[:, 1024:1536]))
                cbd = E("a", lambda a, hq=hq: a.copy(cbb[1][:, hq * 512:(hq + 1) * 512],
                                                     pB[:, 1536:2048]), inc="sA")
                SG(["B2", "B3"], "sA", cbd)
            gate_done = ctx.cnt["sA"]

            h_written = 0
            down_done = 0
            first_y = (True,)
            for e in range(NE):
                for fg in range(FGN):
                    for fs in range(NDC):
                        sl = fs % 2
                        fcol = fg * NDC + fs
                        W("g", "sT", wgu_use[sl])
                        def ldg(g, e=e, fcol=fcol, sl=sl):
                            src = d_wgT[e, :, fcol * 128:(fcol + 1) * 128]
                            src = src.rearrange("(c p) m -> p c m", p=128)
                            dst = wgu[sl][:, :].rearrange("p (c m) -> p c m", c=NDC)
                            return g.dma_start(out=dst, in_=src)
                        ldg_c = E("g", ldg, inc="sWgu%d" % sl, incby=16)
                        W("g", "sT", wgu_use[2 + sl])
                        def ldu(g, e=e, fcol=fcol, sl=sl):
                            src = d_wuT[e, :, fcol * 128:(fcol + 1) * 128]
                            src = src.rearrange("(c p) m -> p c m", p=128)
                            dst = wgu[2 + sl][:, :].rearrange("p (c m) -> p c m", c=NDC)
                            return g.dma_start(out=dst, in_=src)
                        wu_ld = E("g", ldu, inc="sWgu%d" % (2 + sl), incby=16)

                        psg = pA if sl == 0 else pB
                        gb = [("A" if sl == 0 else "B") + str(i) for i in range(2)]
                        ub = [("A" if sl == 0 else "B") + str(i) for i in range(2, 4)]
                        W("t", "sWgu%d" % sl, ldg_c)
                        W("t", "sWgu%d" % (2 + sl), wu_ld)
                        if e == 0 and fg == 0 and fs == 0:
                            W("t", "sA", gate_done)
                            W("v", "sA", gate_done)
                        WB("t", gb)
                        for dc in range(NDC):
                            for nt in range(2):
                                def mmG(t, psg=psg, sl=sl, dc=dc, nt=nt):
                                    return t.matmul(psg[:, nt * 512:(nt + 1) * 512],
                                                    wgu[sl][:, dc * 128:(dc + 1) * 128],
                                                    x1qg[:, dc * QG + nt * 512:dc * QG + (nt + 1) * 512],
                                                    start=(dc == 0), stop=(dc == NDC - 1))
                                g_mm = E("t", mmG,
                                         inc=("sT" if dc == NDC - 1 and nt == 1 else None))
                                if g_mm is not None:
                                    gmm = g_mm
                        WB("t", ub)
                        for dc in range(NDC):
                            for nt in range(2):
                                def mmU(t, psg=psg, sl=sl, dc=dc, nt=nt):
                                    return t.matmul(psg[:, 1024 + nt * 512:1024 + (nt + 1) * 512],
                                                    wgu[2 + sl][:, dc * 128:(dc + 1) * 128],
                                                    x1qg[:, dc * QG + nt * 512:dc * QG + (nt + 1) * 512],
                                                    start=(dc == 0), stop=(dc == NDC - 1))
                                u_mm = E("t", mmU,
                                         inc=("sT" if dc == NDC - 1 and nt == 1 else None))
                                if u_mm is not None:
                                    umm = u_mm
                        wgu_use[sl] = gmm
                        wgu_use[2 + sl] = umm
                        W("a", "sT", gmm)
                        if fs >= 2 or e + fg > 0:
                            W("a", "sV", h_written)
                        sg_done = E("a", lambda a, psg=psg, sl=sl: a.activation(
                            sgt[sl][:, :], psg[:, 0:1024], AF.Silu), inc="sA")
                        SG(gb, "sA", sg_done)
                        W("v", "sA", sg_done)
                        W("v", "sT", umm)
                        if e + fg > 0:
                            W("v", "sT", down_done)
                        def hmul(v, psg=psg, sl=sl, fs=fs):
                            return v.scalar_tensor_tensor(hsb[:, fs * QG:(fs + 1) * QG],
                                                          psg[:, 1024:2048], c_onef[:, 0:1],
                                                          sgt[sl][:, :], OP.mult, OP.mult)
                        h_written = E("v", hmul, inc="sV")
                        SG(ub, "sV", h_written)
                    h_all = h_written
                    for ds in range(NDC):
                        sl = ds % 2
                        W("g", "sT", wdt_use[sl])
                        def ldd(g, e=e, fg=fg, ds=ds, sl=sl):
                            src = d_wdT[e, fg * 2048:(fg + 1) * 2048, ds * 128:(ds + 1) * 128]
                            src = src.rearrange("(c p) m -> p c m", p=128)
                            dst = wdt[sl][:, :].rearrange("p (c m) -> p c m", c=NDC)
                            return g.dma_start(out=dst, in_=src)
                        wd_ld = E("g", ldd, inc="sWdt%d" % sl, incby=16)
                        psd = pB[:, sl * 1024:(sl + 1) * 1024]
                        dbk = ["B%d" % (2 * sl), "B%d" % (2 * sl + 1)]
                        W("t", "sWdt%d" % sl, wd_ld)
                        W("t", "sV", h_all)
                        WB("t", dbk)
                        for fc in range(NDC):
                            for nt in range(2):
                                def mmD(t, psd=psd, sl=sl, fc=fc, nt=nt):
                                    return t.matmul(psd[:, nt * 512:(nt + 1) * 512],
                                                    wdt[sl][:, fc * 128:(fc + 1) * 128],
                                                    hsb[:, fc * QG + nt * 512:fc * QG + (nt + 1) * 512],
                                                    start=(fc == 0), stop=(fc == NDC - 1))
                                E("t", mmD, inc=("sT" if fc == NDC - 1 and nt == 1 else None))
                        wdt_use[sl] = ctx.cnt["sT"]
                        W("v", "sT", ctx.cnt["sT"])
                        if first_y[0]:
                            def yinit(v, psd=psd, ds=ds, e=e):
                                return v.scalar_tensor_tensor(yacc[:, ds * QG:(ds + 1) * QG],
                                                              psd, c_onef[:, 0:1], cbb[e][:, :],
                                                              OP.mult, OP.mult)
                            yd = E("v", yinit, inc="sV")
                        else:
                            def ymul(v, psd=psd, sl=sl, e=e):
                                return v.scalar_tensor_tensor(ytmp[sl][:, :], psd, c_onef[:, 0:1],
                                                              cbb[e][:, :], OP.mult, OP.mult)
                            E("v", ymul)
                            def yadd(v, ds=ds, sl=sl):
                                return v.tensor_add(yacc[:, ds * QG:(ds + 1) * QG],
                                                    yacc[:, ds * QG:(ds + 1) * QG], ytmp[sl][:, :])
                            yd = E("v", yadd, inc="sV")
                        SG(dbk, "sV", yd)
                        down_done = yd
                    first_y = (False,)

            preload(LNEXP_ID)
            y_all = ctx.cnt["sV"]
            if debug and qg == 0:
                W("y", "sV", y_all)
                for dsc in range(NDC):
                    E("y", lambda y, dsc=dsc: y.dma_start(
                        out=d_dby[dsc * 128:(dsc + 1) * 128, :],
                        in_=yacc[:, dsc * QG:(dsc + 1) * QG]), inc="sDbg", incby=16)
                for e_ in range(NE):
                    E("y", lambda y, e_=e_: y.dma_start(out=d_dbc[e_], in_=cbb[e_][:, :]),
                      inc="sDbg", incby=16)
                W("y", "sDbg", ctx.cnt["sDbg"])
                W("v", "sDbg", ctx.cnt["sDbg"])
                W("a", "sDbg", ctx.cnt["sDbg"])
            for qtl in range(2):
                W("a", "sV", y_all)
                res_cnt = [0] * NDC
                for dsc in range(NDC):
                    sl = dsc % 2
                    if dsc >= 2:
                        W("a", "sV", res_cnt[dsc - 2])
                    W("a", "sOut%d" % sl, outst_free[sl])
                    cx = E("a", lambda a, dsc=dsc, qtl=qtl, sl=sl: a.copy(
                        outst[sl][:, :],
                        x1qg[:, dsc * QG + qtl * 512:dsc * QG + (qtl + 1) * 512]), inc="sA")
                    W("v", "sA", cx)
                    def resadd(v, dsc=dsc, qtl=qtl, sl=sl):
                        off = dsc * QG + qtl * 512
                        return v.tensor_add(yacc[:, off:off + 512], yacc[:, off:off + 512],
                                            outst[sl][:, :])
                    res_cnt[dsc] = E("v", resadd, inc="sV")
                res_done = ctx.cnt["sV"]
                W("a", "sV", res_done)
                st_cnt = [0] * NDC
                for dsc in range(NDC):
                    sl = dsc % 2
                    if dsc >= 2:
                        W("a", "sT", st_cnt[dsc - 2])
                    off = dsc * QG + qtl * 512
                    E("a", lambda a, off=off, sl=sl: a.copy(stb[sl][:, :], yacc[:, off:off + 512]))
                    sq = E("a", lambda a, off=off, sl=sl: a.activation(
                        sqb[sl][:, :], yacc[:, off:off + 512], AF.Square), inc="sA")
                    W("t", "sA", sq)
                    if dsc == 0:
                        WB("t", ["A0", "A1"])
                    E("t", lambda t, sl=sl, dsc=dsc: t.matmul(pA[0:1, 0:512], onesk[:, 0:1],
                                                              stb[sl][:, :], start=(dsc == 0),
                                                              stop=(dsc == NDC - 1)))
                    st_cnt[dsc] = E("t", lambda t, sl=sl, dsc=dsc: t.matmul(
                        pA[0:1, 512:1024], onesk[:, 0:1], sqb[sl][:, :],
                        start=(dsc == 0), stop=(dsc == NDC - 1)), inc="sT")
                W("v", "sT", st_cnt[NDC - 1])
                bb_done = ln_rows_and_bcast()
                W("v", "sA", bb_done)
                for dsc in range(NDC):
                    sl = dsc % 2
                    off = dsc * QG + qtl * 512
                    W("v", "sOut%d" % sl, outst_free[sl])
                    E("v", lambda v, off=off: v.tensor_mul(ttb2[:, :], yacc[:, off:off + 512],
                                                           bb0[:, :]))
                    nv = E("v", lambda v, sl=sl: v.tensor_sub(outst[sl][:, :], ttb2[:, :],
                                                              bb1[:, :]), inc="sV")
                    W("y", "sV", nv)
                    def stout(y, dsc=dsc, qg=qg, qtl=qtl, sl=sl):
                        col = qg * QG + qtl * 512
                        return y.dma_start(out=d_out[dsc * 128:(dsc + 1) * 128, col:col + 512],
                                           in_=outst[sl][:, :])
                    outst_free[sl] = E("y", stout, inc="sOut%d" % sl, incby=16)

        W("y", "sOut0", ctx.cnt["sOut0"])
        W("y", "sOut1", ctx.cnt["sOut1"])
        ctx.flush()
        nc._dbg_cnt = dict(ctx.cnt)

        for cm in reversed(ctx.sem_cms):
            cm.__exit__(None, None, None)

    for cm in reversed(open_cms):
        cm.__exit__(None, None, None)
    return nc



def _own_token_ranges(half):
    return [(qi * QT, (qi + 1) * QT) for qi in SLOT_SETS[half]]


def _thr_host(half):
    thr = np.full((128, 128), 1000.0, dtype=np.float32)
    p = np.arange(128, dtype=np.float32)
    for j, qi in enumerate(SLOT_SETS[half]):
        g = qi * QT
        for kb in range(NMAX[j]):
            if kb < NREAL[half][j]:
                thr[:, j * 32 + kb] = 128.0 * kb + p - g
    return thr


_NC_CACHE = {}


def kernel(L0_attn_x, L0_attn_W_qkv, L0_attn_W_proj, L0_ffn_gate_w,
           L0_ffn_w_gate, L0_ffn_w_up, L0_ffn_w_down):
    x = np.asarray(L0_attn_x, dtype=np.float32)
    Wqkv = np.asarray(L0_attn_W_qkv, dtype=np.float32)
    Wproj = np.asarray(L0_attn_W_proj, dtype=np.float32)
    gate_w = np.asarray(L0_ffn_gate_w, dtype=np.float32)
    w_gate = np.asarray(L0_ffn_w_gate, dtype=np.float32)
    w_up = np.asarray(L0_ffn_w_up, dtype=np.float32)
    w_down = np.asarray(L0_ffn_w_down, dtype=np.float32)

    wqT = np.ascontiguousarray(Wqkv[0:D].T)
    wkT = np.ascontiguousarray(Wqkv[D:2 * D].T)
    wvT = np.ascontiguousarray(Wqkv[2 * D:3 * D].T)
    wpT = np.ascontiguousarray(Wproj.T)
    gwT = np.ascontiguousarray(gate_w.T)
    wgT = np.ascontiguousarray(np.transpose(w_gate, (0, 2, 1)))
    wuT = np.ascontiguousarray(np.transpose(w_up, (0, 2, 1)))
    wdT = np.ascontiguousarray(np.transpose(w_down, (0, 2, 1)))
    iota = np.ascontiguousarray(np.broadcast_to(np.arange(QT, dtype=np.float32), (128, QT)))
    thr = [_thr_host(0), _thr_host(1)]
    xT = [np.ascontiguousarray(x[b].T) for b in range(B)]

    in_maps = []
    for c in range(NC := 8):
        b, half = c // 2, c % 2
        xo = np.ascontiguousarray(
            np.concatenate([x[b][r0:r1] for (r0, r1) in _own_token_ranges(half)], axis=0).T)
        in_maps.append({
            "xT": xT[b], "xown": xo, "thr": thr[half], "iota": iota,
            "wqT": wqT, "wkT": wkT, "wvT": wvT, "wpT": wpT, "gwT": gwT,
            "wgT": wgT, "wuT": wuT, "wdT": wdT,
        })

    if "nc" not in _NC_CACHE:
        _NC_CACHE["nc"] = build_nc()
    res = bass_utils.run_bass_kernel_spmd(_NC_CACHE["nc"], in_maps, core_ids=list(range(8)))

    out = np.empty((B, S, D), dtype=np.float32)
    for c in range(8):
        b, half = c // 2, c % 2
        oT = np.asarray(res.results[c]["out"]).T
        for j, (r0, r1) in enumerate(_own_token_ranges(half)):
            out[b, r0:r1] = oT[j * QT:(j + 1) * QT]
    return out


# version 05
# speedup vs baseline: 1.0037x; 1.0037x over previous
import math
import numpy as np

import concourse.bass as bass
import concourse.mybir as mybir
from concourse import bass_utils
from concourse import compiler_utils as _cu

_flags = _cu.get_compiler_flags()
_flags = [f.replace("--enable-ldw-opt=false", "--enable-ldw-opt=true") for f in _flags]
_cu.set_compiler_flags(_flags)

F32 = mybir.dt.float32
BF16 = mybir.dt.bfloat16
AF = mybir.ActivationFunctionType
OP = mybir.AluOpType

D, S, B, H, HD, FF, NE = 2048, 4096, 4, 16, 128, 8192, 2
QT, OWN, NDC = 512, 2048, 16
SLOT_SETS = [[7, 4, 3, 0], [6, 5, 2, 1]]
NREAL = [[32, 20, 16, 4], [28, 24, 12, 8]]
NMAX = [32, 24, 16, 8]
SCALE = 1.0 / math.sqrt(2048.0)
EXPB = -4.0
EPS = 1e-5
QG, NQG = 1024, 2
FGN = 4


class Ctx:
    ENG = {"g": "gpsimd", "v": "vector", "t": "tensor", "a": "scalar", "y": "sync"}

    def __init__(self, nc, block):
        self.nc, self.block = nc, block
        self.q = {e: [] for e in self.ENG}
        self.sem, self.cnt = {}, {}
        self.guard = {}
        self.sem_cms = []

    def sema(self, name):
        cm = self.nc.semaphore(name)
        self.sem[name] = cm.__enter__()
        self.sem_cms.append(cm)
        self.cnt[name] = 0

    def emit(self, e, fn, inc=None, incby=1):
        h = None
        if inc is not None:
            self.cnt[inc] += incby
            h = self.sem[inc]
        drained = inc is not None and e in ("v", "a")

        def f(eng, fn=fn, h=h, incby=incby, drained=drained):
            ins = fn(eng)
            if h is not None:
                if drained:
                    eng.drain().then_inc(h, incby)
                else:
                    ins.then_inc(h, incby)
        self.q[e].append(f)
        return self.cnt[inc] if inc is not None else None

    def wait(self, e, name, val):
        if not val or val <= 0:
            return
        h = self.sem[name]
        self.q[e].append(lambda eng, h=h, val=val: eng.wait_ge(h, val))

    def wait_banks(self, e, banks):
        need = {}
        for b in banks:
            if b in self.guard:
                sn, v = self.guard[b]
                need[sn] = max(need.get(sn, 0), v)
        for sn, v in need.items():
            self.wait(e, sn, v)

    def set_guard(self, banks, semname, val):
        for b in banks:
            self.guard[b] = (semname, val)

    def barrier(self):
        for e in self.ENG:
            for sn in self.sem:
                self.wait(e, sn, self.cnt[sn])

    def flush(self):
        for e, lst in self.q.items():
            if not lst:
                continue

            def body(eng, lst=tuple(lst)):
                for f in lst:
                    f(eng)
            getattr(self.block, self.ENG[e])(body)
        self.q = {e: [] for e in self.ENG}


def build_nc(debug=False):
    nc = bass.Bass("TRN2", target_bir_lowering=False)

    d_xT = nc.dram_tensor("xT", [D, S], F32, kind="ExternalInput").ap()
    d_xo = nc.dram_tensor("xown", [D, OWN], F32, kind="ExternalInput").ap()
    d_thr = nc.dram_tensor("thr", [128, 128], F32, kind="ExternalInput").ap()
    d_iota = nc.dram_tensor("iota", [128, QT], F32, kind="ExternalInput").ap()
    d_wqT = nc.dram_tensor("wqT", [D, D], F32, kind="ExternalInput").ap()
    d_wkT = nc.dram_tensor("wkT", [D, D], F32, kind="ExternalInput").ap()
    d_wvT = nc.dram_tensor("wvT", [D, D], F32, kind="ExternalInput").ap()
    d_wpT = nc.dram_tensor("wpT", [D, D], F32, kind="ExternalInput").ap()
    d_gwT = nc.dram_tensor("gwT", [D, NE], F32, kind="ExternalInput").ap()
    d_wgT = nc.dram_tensor("wgT", [NE, D, FF], F32, kind="ExternalInput").ap()
    d_wuT = nc.dram_tensor("wuT", [NE, D, FF], F32, kind="ExternalInput").ap()
    d_wdT = nc.dram_tensor("wdT", [NE, FF, D], F32, kind="ExternalInput").ap()
    d_out = nc.dram_tensor("out", [D, OWN], F32, kind="ExternalOutput").ap()
    scr_kind = "ExternalOutput" if debug else "Internal"
    d_kscr = nc.dram_tensor("kscr", [D, S], BF16, kind=scr_kind).ap()
    d_vscr = nc.dram_tensor("vscr", [H, S, HD], BF16, kind=scr_kind).ap()
    d_x1scr = nc.dram_tensor("x1scr", [D, OWN], BF16, kind=scr_kind).ap()
    d_dbq = nc.dram_tensor("dbq", [D, OWN], BF16, kind="ExternalOutput").ap() if debug else None
    d_dbo = nc.dram_tensor("dbo", [D, OWN], BF16, kind="ExternalOutput").ap() if debug else None
    d_dba = nc.dram_tensor("dba", [D, QT], F32, kind="ExternalOutput").ap() if debug else None
    d_dbx = nc.dram_tensor("dbx", [128, QT], F32, kind="ExternalOutput").ap() if debug else None
    d_dby = nc.dram_tensor("dby", [D, QG], F32, kind="ExternalOutput").ap() if debug else None
    d_dbc = nc.dram_tensor("dbc", [NE, 128, QG], F32, kind="ExternalOutput").ap() if debug else None
    d_dbb = nc.dram_tensor("dbb", [2, 128, QT], F32, kind="ExternalOutput").ap() if debug else None

    open_cms = []

    def alloc(name, shape, dtype, side="left", psum=False):
        cm = nc.psum_tensor(name, shape, dtype) if psum else nc.sbuf_tensor(name, shape, dtype, side=side)
        h = cm.__enter__()
        open_cms.append(cm)
        return h, cm

    def free(*cmlist):
        for cm in cmlist:
            cm.__exit__(None, None, None)
            open_cms.remove(cm)

    from concourse.hw_specs import get_activation_tables
    act_sets = list(get_activation_tables(nc.m.arch))
    LNEXP_ID = act_sets.index("natural_log_exp_and_others")
    SILU_ID = act_sets.index("silu_and_others")

    iota_sb, _ = alloc("iota_sb", [128, QT], F32)
    thr_sb, _ = alloc("thr_sb", [128, 128], F32)
    onesk, _ = alloc("onesk", [128, 1], BF16)
    ones1, _ = alloc("ones1", [1, 128], BF16)
    c_scale, _ = alloc("c_scale", [128, 1], F32)
    c_expb, _ = alloc("c_expb", [128, 1], F32)
    c_onef, _ = alloc("c_onef", [128, 1], F32)
    c_eps, _ = alloc("c_eps", [128, 1], F32)
    c_mhalf, _ = alloc("c_mhalf", [128, 1], F32)
    c_mone, _ = alloc("c_mone", [128, 1], F32)
    wst = [alloc(f"wst{i}", [128, NDC * 128], BF16)[0] for i in range(2)]
    row_a, _ = alloc("row_a", [1, QT], F32)
    row_b, _ = alloc("row_b", [1, QT], F32)
    row_c, _ = alloc("row_c", [1, QT], F32)
    row_d, _ = alloc("row_d", [1, QT], F32)
    row_bf, _ = alloc("row_bf", [1, QT], BF16)
    row_bf2, _ = alloc("row_bf2", [1, QT], BF16)
    bb0, _ = alloc("bb0", [128, QT], F32)
    bb1, _ = alloc("bb1", [128, QT], F32)
    stb = [alloc(f"stb{i}", [128, QT], BF16)[0] for i in range(2)]
    sqb = [alloc(f"sqb{i}", [128, QT], BF16)[0] for i in range(2)]

    pA, _ = alloc("pA", [128, 2048], F32, psum=True)
    pB, _ = alloc("pB", [128, 2048], F32, psum=True)

    with nc.Block() as block:
        ctx = Ctx(nc, block)
        for sname in ("sT", "sV", "sA", "sInit", "sXb", "sWvh", "sDbg",
                      "sWst0", "sWst1", "sKst0", "sKst1", "sVstg0", "sVstg1",
                      "sKvh0", "sKvh1", "sXres0", "sXres1", "sX1st0", "sX1st1",
                      "sOut0", "sOut1", "sWdt0", "sWdt1",
                      "sWgu0", "sWgu1", "sWgu2", "sWgu3"):
            ctx.sema(sname)
        E, W, WB, SG = ctx.emit, ctx.wait, ctx.wait_banks, ctx.set_guard

        def preload(set_id):
            def f(a, set_id=set_id):
                return a.add_instruction(mybir.InstLoadActFuncSet(
                    name=nc.get_next_instruction_name(), act_func_set_id=set_id,
                    ins=[], outs=[]))
            E("a", f)

        for buf, val in ((onesk, 1.0), (ones1, 1.0), (c_scale, SCALE), (c_expb, EXPB),
                         (c_onef, 1.0), (c_eps, EPS), (c_mhalf, -0.5), (c_mone, -1.0)):
            E("g", lambda g, b=buf, v=val: g.memset(b[:, :], v))
        E("g", lambda g: g.dma_start(out=iota_sb[:, :], in_=d_iota), inc="sInit", incby=16)
        init_done = E("g", lambda g: g.dma_start(out=thr_sb[:, :], in_=d_thr), inc="sInit", incby=16)

        wst_use = [0, 0]

        def load_wst(dram, col, sl):
            W("g", "sT", wst_use[sl])
            def ld(g, dram=dram, col=col, sl=sl):
                src = dram[:, col * 128:(col + 1) * 128].rearrange("(c p) m -> p c m", p=128)
                dst = wst[sl][:, :].rearrange("p (c m) -> p c m", c=NDC)
                return g.dma_start(out=dst, in_=src)
            return E("g", ld, inc="sWst%d" % sl, incby=16)

        kst = [alloc(f"kst{i}", [128, 2048], BF16)[0] for i in range(2)]
        kst_cms = open_cms[-2:]
        vstg = [alloc(f"vstg{i}", [128, 1024], BF16)[0] for i in range(2)]
        vstg_cms = open_cms[-2:]
        wvh, wvh_cm = alloc("wvh", [128, NDC * 1024], BF16)
        xb, xb_cm = alloc("xb", [128, NDC * 2048], BF16, side="right")

        kst_free = [0, 0]
        vstg_free = [0, 0]

        for tg in range(2):
            W("g", "sT", ctx.cnt["sT"])
            xb_ld = 0
            for i in range(4):
                def ld_xb(g, i=i, tg=tg):
                    src = d_xT[i * 512:(i + 1) * 512, tg * 2048:(tg + 1) * 2048]
                    src = src.rearrange("(c p) t -> p c t", p=128)
                    dst = xb[:, i * 8192:(i + 1) * 8192].rearrange("p (c t) -> p c t", c=4)
                    return g.dma_start(out=dst, in_=src)
                xb_ld = E("g", ld_xb, inc="sXb", incby=16)

            for ks in range(NDC):
                sl = ks % 2
                wld = load_wst(d_wkT, ks, sl)
                ps = pA if sl == 0 else pB
                banks = [("A" if sl == 0 else "B") + str(i) for i in range(4)]
                W("t", "sWst%d" % sl, wld)
                W("t", "sXb", xb_ld)
                W("t", "sInit", init_done)
                WB("t", banks)
                for dc in range(NDC):
                    for nt in range(4):
                        def mm(t, ps=ps, sl=sl, dc=dc, nt=nt):
                            return t.matmul(ps[:, nt * 512:(nt + 1) * 512],
                                            wst[sl][:, dc * 128:(dc + 1) * 128],
                                            xb[:, dc * 2048 + nt * 512:dc * 2048 + (nt + 1) * 512],
                                            start=(dc == 0), stop=(dc == NDC - 1))
                        E("t", mm, inc=("sT" if dc == NDC - 1 and nt == 3 else None))
                wst_use[sl] = ctx.cnt["sT"]
                W("v", "sT", ctx.cnt["sT"])
                W("v", "sKst%d" % sl, kst_free[sl])
                ev = E("v", lambda v, ps=ps, sl=sl: v.tensor_copy(kst[sl][:, :], ps[:, :]), inc="sV")
                SG(banks, "sV", ev)
                W("y", "sV", ev)
                def stk(y, ks=ks, sl=sl, tg=tg):
                    return y.dma_start(out=d_kscr[ks * 128:(ks + 1) * 128, tg * 2048:(tg + 1) * 2048],
                                       in_=kst[sl][:, :])
                kst_free[sl] = E("y", stk, inc="sKst%d" % sl, incby=16)

            for vh2 in range(2):
                W("g", "sT", ctx.cnt["sT"])
                def ld_wv(g, vh2=vh2):
                    src = d_wvT[:, vh2 * 1024:(vh2 + 1) * 1024]
                    src = src.rearrange("(c p) m -> p c m", p=128)
                    dst = wvh[:, :].rearrange("p (c m) -> p c m", c=NDC)
                    return g.dma_start(out=dst, in_=src)
                wv_ld = E("g", ld_wv, inc="sWvh", incby=16)
                for tb in range(16):
                    sl = tb % 2
                    ps = pA if sl == 0 else pB
                    banks = [("A" if sl == 0 else "B") + str(i) for i in range(2)]
                    W("t", "sWvh", wv_ld)
                    WB("t", banks)
                    for dc in range(NDC):
                        for nt in range(2):
                            def mmv(t, ps=ps, dc=dc, nt=nt, tb=tb):
                                return t.matmul(ps[:, nt * 512:(nt + 1) * 512],
                                                xb[:, dc * 2048 + tb * 128:dc * 2048 + (tb + 1) * 128],
                                                wvh[:, dc * 1024 + nt * 512:dc * 1024 + (nt + 1) * 512],
                                                start=(dc == 0), stop=(dc == NDC - 1))
                            E("t", mmv, inc=("sT" if dc == NDC - 1 and nt == 1 else None))
                    W("v", "sT", ctx.cnt["sT"])
                    W("v", "sVstg%d" % sl, vstg_free[sl])
                    ev = E("v", lambda v, ps=ps, sl=sl: v.tensor_copy(vstg[sl][:, :], ps[:, 0:1024]),
                           inc="sV")
                    SG(banks, "sV", ev)
                    W("y", "sV", ev)
                    def stv(y, vh2=vh2, tg=tg, tb=tb, sl=sl):
                        dst = d_vscr[vh2 * 8:(vh2 + 1) * 8,
                                     tg * 2048 + tb * 128:tg * 2048 + (tb + 1) * 128, :]
                        dst = dst.rearrange("h t v -> t h v")
                        srcv = vstg[sl][:, :].rearrange("t (h v) -> t h v", h=8)
                        return y.dma_start(out=dst, in_=srcv)
                    vstg_free[sl] = E("y", stv, inc="sVstg%d" % sl, incby=16)
        kv_store_total = {n: ctx.cnt[n] for n in ("sKst0", "sKst1", "sVstg0", "sVstg1")}
        ctx.flush()
        free(wvh_cm, *reversed(vstg_cms), *reversed(kst_cms))

        qsb, qsb_cm = alloc("qsb", [128, H * 2048], BF16)
        W("g", "sT", ctx.cnt["sT"])
        xb_ld = 0
        for i in range(4):
            def ld_xo(g, i=i):
                src = d_xo[i * 512:(i + 1) * 512, :].rearrange("(c p) t -> p c t", p=128)
                dst = xb[:, i * 8192:(i + 1) * 8192].rearrange("p (c t) -> p c t", c=4)
                return g.dma_start(out=dst, in_=src)
            xb_ld = E("g", ld_xo, inc="sXb", incby=16)
        for qs in range(NDC):
            sl = qs % 2
            wld = load_wst(d_wqT, qs, sl)
            ps = pA if sl == 0 else pB
            banks = [("A" if sl == 0 else "B") + str(i) for i in range(4)]
            W("t", "sWst%d" % sl, wld)
            W("t", "sXb", xb_ld)
            WB("t", banks)
            for dc in range(NDC):
                for nt in range(4):
                    def mmq(t, ps=ps, sl=sl, dc=dc, nt=nt):
                        return t.matmul(ps[:, nt * 512:(nt + 1) * 512],
                                        wst[sl][:, dc * 128:(dc + 1) * 128],
                                        xb[:, dc * 2048 + nt * 512:dc * 2048 + (nt + 1) * 512],
                                        start=(dc == 0), stop=(dc == NDC - 1))
                    E("t", mmq, inc=("sT" if dc == NDC - 1 and nt == 3 else None))
            wst_use[sl] = ctx.cnt["sT"]
            W("v", "sT", ctx.cnt["sT"])
            if qs == 0:
                for n_, v_ in kv_store_total.items():
                    W("v", n_, v_)
            ev = E("v", lambda v, ps=ps, qs=qs: v.tensor_copy(qsb[:, qs * 2048:(qs + 1) * 2048],
                                                              ps[:, :]), inc="sV")
            SG(banks, "sV", ev)
        q_done = ctx.cnt["sV"]
        if debug:
            W("y", "sV", q_done)
            for qs in range(NDC):
                E("y", lambda y, qs=qs: y.dma_start(
                    out=d_dbq[qs * 128:(qs + 1) * 128, :],
                    in_=qsb[:, qs * 2048:(qs + 1) * 2048]), inc="sDbg", incby=16)
        ctx.flush()
        free(xb_cm)

        khb = [alloc(f"khb{i}", [128, S], BF16)[0] for i in range(2)]
        khb_cms = open_cms[-2:]
        vhb = [alloc(f"vhb{i}", [128, S], BF16)[0] for i in range(2)]
        vhb_cms = open_cms[-2:]
        ptile = [alloc(f"ptile{i}", [128, QT], BF16)[0] for i in range(4)]
        pt_cms = open_cms[-4:]
        pmtile = [alloc(f"pmtile{i}", [128, QT], BF16)[0] for i in range(4)]
        pm_cms = open_cms[-4:]
        osb, osb_cm = alloc("osb", [128, H * 2048], BF16, side="right")

        preload(LNEXP_ID)

        head_free = [0, 0]
        kvh_ld = [0, 0]
        last_bcast = 0
        last_onorm = 0
        for h in range(H):
            sl = h % 2
            for n_, v_ in kv_store_total.items():
                W("y", n_, v_)
            W("y", "sT", head_free[sl])
            E("y", lambda y, h=h, sl=sl: y.dma_start(out=khb[sl][:, :],
                                                     in_=d_kscr[h * 128:(h + 1) * 128, :]),
              inc="sKvh%d" % sl, incby=16)
            def ldv(y, h=h, sl=sl):
                src = d_vscr[h].rearrange("(kb p) v -> p kb v", p=128)
                dst = vhb[sl][:, :].rearrange("p (kb v) -> p kb v", kb=32)
                return y.dma_start(out=dst, in_=src)
            kvh_ld[sl] = E("y", ldv, inc="sKvh%d" % sl, incby=16)

            W("t", "sKvh%d" % sl, kvh_ld[sl])
            if h == 0:
                W("t", "sV", q_done)
            for j in range(4):
                n = NMAX[j]
                jp = j % 2
                o_ps = pA[:, jp * 512:(jp + 1) * 512]
                o_banks = ["A%d" % jp]
                sum_ps = pA[0:1, 1024 + jp * 512:1024 + (jp + 1) * 512]
                sum_banks = ["A%d" % (2 + jp)]
                exp_cnt = [0] * n
                mask_cnt = [0] * n
                pv_cnt = [0] * n

                def emit_pv(kb):
                    W("t", "sV", mask_cnt[kb])
                    def mmPV(t, h=h, kb=kb, n=n, o_ps=o_ps):
                        return t.matmul(o_ps, vhb[h % 2][:, kb * 128:(kb + 1) * 128],
                                        pmtile[kb % 4][:, :], start=(kb == 0), stop=(kb == n - 1))
                    E("t", mmPV)
                    def mmSU(t, kb=kb, n=n, sum_ps=sum_ps):
                        return t.matmul(sum_ps, onesk[:, 0:1], pmtile[kb % 4][:, :],
                                        start=(kb == 0), stop=(kb == n - 1))
                    pv_cnt[kb] = E("t", mmSU, inc="sT")

                for kb in range(n):
                    ksl = kb % 4
                    s_ps = pB[:, ksl * 512:(ksl + 1) * 512]
                    s_banks = ["B%d" % ksl]
                    WB("t", s_banks)
                    if kb == 0:
                        WB("t", o_banks)
                        WB("t", sum_banks)
                    def mmS(t, h=h, j=j, kb=kb, s_ps=s_ps):
                        return t.matmul(s_ps, khb[h % 2][:, kb * 128:(kb + 1) * 128],
                                        qsb[:, h * 2048 + j * 512:h * 2048 + (j + 1) * 512],
                                        start=True, stop=True)
                    s_cnt = E("t", mmS, inc="sT")
                    W("a", "sT", s_cnt)
                    if kb >= 4:
                        W("a", "sV", mask_cnt[kb - 4])
                    def act_exp(a, s_ps=s_ps, ksl=ksl):
                        return a.activation(ptile[ksl][:, :], s_ps, AF.Exp,
                                            bias=c_expb[:, 0:1], scale=c_scale[:, 0:1])
                    exp_cnt[kb] = E("a", act_exp, inc="sA")
                    SG(s_banks, "sA", exp_cnt[kb])
                    W("v", "sA", exp_cnt[kb])
                    if kb >= 4:
                        W("v", "sT", pv_cnt[kb - 4])
                    col = j * 32 + kb
                    def msk(v, col=col, ksl=ksl):
                        return v.scalar_tensor_tensor(pmtile[ksl][:, :], iota_sb[:, :],
                                                      thr_sb[:, col:col + 1], ptile[ksl][:, :],
                                                      OP.is_ge, OP.mult)
                    mask_cnt[kb] = E("v", msk, inc="sV")
                    if kb >= 2:
                        emit_pv(kb - 2)
                emit_pv(n - 2)
                emit_pv(n - 1)

                W("a", "sT", pv_cnt[n - 1])
                W("a", "sT", last_bcast)
                E("a", lambda a, sum_ps=sum_ps: a.activation(row_a[:, :], sum_ps, AF.Ln))
                E("a", lambda a: a.activation(row_b[:, :], row_a[:, :], AF.Exp,
                                              scale=c_mone[0:1, 0:1]))
                rb_done = E("a", lambda a: a.copy(row_bf[:, :], row_b[:, :]), inc="sA")
                SG(sum_banks, "sA", rb_done)
                bc_ps = pB[:, 1024 + jp * 512:1024 + (jp + 1) * 512]
                bc_banks = ["B%d" % (2 + jp)]
                W("t", "sA", rb_done)
                WB("t", bc_banks)
                last_bcast = E("t", lambda t, bc_ps=bc_ps: t.matmul(
                    bc_ps, ones1[0:1, :], row_bf[0:1, :], start=True, stop=True), inc="sT")
                W("a", "sT", last_bcast)
                W("a", "sV", last_onorm)
                rbb = E("a", lambda a, bc_ps=bc_ps: a.copy(bb0[:, :], bc_ps), inc="sA")
                SG(bc_banks, "sA", rbb)
                W("v", "sA", rbb)
                def onorm(v, h=h, j=j, o_ps=o_ps):
                    return v.scalar_tensor_tensor(
                        osb[:, h * 2048 + j * 512:h * 2048 + (j + 1) * 512],
                        o_ps, c_onef[:, 0:1], bb0[:, :], OP.mult, OP.mult)
                last_onorm = E("v", onorm, inc="sV")
                SG(o_banks, "sV", last_onorm)
            head_free[sl] = ctx.cnt["sT"]
        o_done = ctx.cnt["sV"]
        if debug:
            W("y", "sV", o_done)
            for qs in range(NDC):
                E("y", lambda y, qs=qs: y.dma_start(
                    out=d_dbo[qs * 128:(qs + 1) * 128, :],
                    in_=osb[:, qs * 2048:(qs + 1) * 2048]), inc="sDbg", incby=16)
        ctx.flush()
        free(*reversed(pm_cms), *reversed(pt_cms), *reversed(vhb_cms), *reversed(khb_cms), qsb_cm)

        a1b, a1b_cm = alloc("a1b", [128, NDC * 512], F32)
        xres = [alloc(f"xres{i}", [128, QT], F32)[0] for i in range(2)]
        xres_cms = open_cms[-2:]
        x1st = [alloc(f"x1st{i}", [128, QT], BF16)[0] for i in range(2)]
        x1st_cms = open_cms[-2:]
        ttb, ttb_cm = alloc("ttb", [128, QT], F32)

        xres_free = [0, 0]
        x1_free = [0, 0]
        prev_stats_sA = 0

        def ln_rows_and_bcast():
            E("v", lambda v: v.tensor_scalar(row_a[:, :], pA[0:1, 0:512], 1.0 / D, None, OP.mult))
            E("v", lambda v: v.tensor_mul(row_b[:, :], row_a[:, :], row_a[:, :]))
            vd = E("v", lambda v: v.scalar_tensor_tensor(row_c[:, :], pA[0:1, 512:1024],
                                                         1.0 / D, row_b[:, :],
                                                         OP.mult, OP.subtract), inc="sV")
            SG(["A0", "A1"], "sV", vd)
            W("a", "sV", vd)
            E("a", lambda a: a.activation(row_c[:, :], row_c[:, :], AF.Ln, bias=c_eps[0:1, 0:1]))
            rsd = E("a", lambda a: a.activation(row_c[:, :], row_c[:, :], AF.Exp,
                                                scale=c_mhalf[0:1, 0:1]), inc="sA")
            W("v", "sA", rsd)
            E("v", lambda v: v.tensor_mul(row_d[:, :], row_a[:, :], row_c[:, :]))
            E("v", lambda v: v.tensor_copy(row_bf[:, :], row_c[:, :]))
            mrd = E("v", lambda v: v.tensor_copy(row_bf2[:, :], row_d[:, :]), inc="sV")
            W("t", "sV", mrd)
            WB("t", ["A2", "A3"])
            E("t", lambda t: t.matmul(pA[:, 1024:1536], ones1[0:1, :], row_bf[0:1, :],
                                      start=True, stop=True))
            b2 = E("t", lambda t: t.matmul(pA[:, 1536:2048], ones1[0:1, :], row_bf2[0:1, :],
                                           start=True, stop=True), inc="sT")
            W("a", "sT", b2)
            E("a", lambda a: a.copy(bb0[:, :], pA[:, 1024:1536]))
            cpd = E("a", lambda a: a.copy(bb1[:, :], pA[:, 1536:2048]), inc="sA")
            SG(["A2", "A3"], "sA", cpd)
            return cpd

        W("y", "sT", ctx.cnt["sT"])
        W("y", "sV", o_done)
        for qt in range(4):
            for ds in range(NDC):
                sl = ds % 2
                wld = load_wst(d_wpT, ds, sl)
                W("y", "sV", xres_free[sl])
                xr = E("y", lambda y, ds=ds, qt=qt, sl=sl: y.dma_start(
                    out=xres[sl][:, :],
                    in_=d_xo[ds * 128:(ds + 1) * 128, qt * 512:(qt + 1) * 512]),
                    inc="sXres%d" % sl, incby=16)
                ps = pB[:, sl * 512:(sl + 1) * 512]
                banks = ["B%d" % sl]
                W("t", "sWst%d" % sl, wld)
                if qt == 0 and ds == 0:
                    W("t", "sV", o_done)
                WB("t", banks)
                for dc in range(NDC):
                    def mmp(t, ps=ps, sl=sl, dc=dc, qt=qt):
                        return t.matmul(ps, wst[sl][:, dc * 128:(dc + 1) * 128],
                                        osb[:, dc * 2048 + qt * 512:dc * 2048 + (qt + 1) * 512],
                                        start=(dc == 0), stop=(dc == NDC - 1))
                    E("t", mmp, inc=("sT" if dc == NDC - 1 else None))
                wst_use[sl] = ctx.cnt["sT"]
                W("v", "sT", ctx.cnt["sT"])
                W("v", "sXres%d" % sl, xr)
                W("v", "sA", prev_stats_sA)
                ad = E("v", lambda v, ps=ps, ds=ds, sl=sl: v.tensor_add(
                    a1b[:, ds * 512:(ds + 1) * 512], ps, xres[sl][:, :]), inc="sV")
                xres_free[sl] = ad
                SG(banks, "sV", ad)
                if debug and qt == 0 and ds == 0:
                    W("y", "sV", ad)
                    E("y", lambda y: y.dma_start(out=d_dbx, in_=xres[0][:, :]),
                      inc="sDbg", incby=16)
                    W("y", "sDbg", ctx.cnt["sDbg"])
            a1_done = ctx.cnt["sV"]

            W("a", "sV", a1_done)
            st_cnt = [0] * NDC
            for dsc in range(NDC):
                sl = dsc % 2
                if dsc >= 2:
                    W("a", "sT", st_cnt[dsc - 2])
                E("a", lambda a, dsc=dsc, sl=sl: a.copy(stb[sl][:, :],
                                                        a1b[:, dsc * 512:(dsc + 1) * 512]))
                sq = E("a", lambda a, dsc=dsc, sl=sl: a.activation(
                    sqb[sl][:, :], a1b[:, dsc * 512:(dsc + 1) * 512], AF.Square), inc="sA")
                W("t", "sA", sq)
                if dsc == 0:
                    WB("t", ["A0", "A1"])
                E("t", lambda t, sl=sl, dsc=dsc: t.matmul(pA[0:1, 0:512], onesk[:, 0:1],
                                                          stb[sl][:, :], start=(dsc == 0),
                                                          stop=(dsc == NDC - 1)))
                st_cnt[dsc] = E("t", lambda t, sl=sl, dsc=dsc: t.matmul(
                    pA[0:1, 512:1024], onesk[:, 0:1], sqb[sl][:, :],
                    start=(dsc == 0), stop=(dsc == NDC - 1)), inc="sT")
            prev_stats_sA = ctx.cnt["sA"]
            if debug and qt == 0:
                W("y", "sV", a1_done)
                for dsc in range(NDC):
                    E("y", lambda y, dsc=dsc: y.dma_start(
                        out=d_dba[dsc * 128:(dsc + 1) * 128, :],
                        in_=a1b[:, dsc * 512:(dsc + 1) * 512]), inc="sDbg", incby=16)
                W("v", "sDbg", ctx.cnt["sDbg"])
                W("a", "sDbg", ctx.cnt["sDbg"])
            W("v", "sT", st_cnt[NDC - 1])
            bb_done = ln_rows_and_bcast()
            W("v", "sA", bb_done)
            if debug and qt == 0:
                W("y", "sA", bb_done)
                E("y", lambda y: y.dma_start(out=d_dbb[0], in_=bb0[:, :]), inc="sDbg", incby=16)
                E("y", lambda y: y.dma_start(out=d_dbb[1], in_=bb1[:, :]), inc="sDbg", incby=16)
                W("v", "sDbg", ctx.cnt["sDbg"])
                W("a", "sDbg", ctx.cnt["sDbg"])
            for dsc in range(NDC):
                sl = dsc % 2
                W("v", "sX1st%d" % sl, x1_free[sl])
                E("v", lambda v, dsc=dsc: v.tensor_mul(ttb[:, :],
                                                       a1b[:, dsc * 512:(dsc + 1) * 512], bb0[:, :]))
                nv = E("v", lambda v, sl=sl: v.tensor_sub(x1st[sl][:, :], ttb[:, :], bb1[:, :]),
                       inc="sV")
                W("y", "sV", nv)
                x1_free[sl] = E("y", lambda y, dsc=dsc, qt=qt, sl=sl: y.dma_start(
                    out=d_x1scr[dsc * 128:(dsc + 1) * 128, qt * 512:(qt + 1) * 512],
                    in_=x1st[sl][:, :]), inc="sX1st%d" % sl, incby=16)
        x1_total = {n: ctx.cnt[n] for n in ("sX1st0", "sX1st1")}
        ctx.flush()
        free(ttb_cm, *reversed(x1st_cms), *reversed(xres_cms), a1b_cm, osb_cm)

        x1qg, _ = alloc("x1qg", [128, NDC * QG], BF16)
        yacc, _ = alloc("yacc", [128, NDC * QG], F32)
        hsb, _ = alloc("hsb", [128, NDC * QG], BF16)
        wgu = [alloc(f"wgu{i}", [128, NDC * 128], BF16)[0] for i in range(4)]
        wdt = [alloc(f"wdt{i}", [128, NDC * 128], BF16)[0] for i in range(2)]
        cbb = [alloc(f"cbb{i}", [128, QG], F32)[0] for i in range(2)]
        sgt = [alloc(f"sgt{i}", [128, QG], BF16)[0] for i in range(2)]
        ytmp = [alloc(f"ytmp{i}", [128, QG], F32)[0] for i in range(2)]
        outst = [alloc(f"outst{i}", [128, QT], F32)[0] for i in range(2)]
        ttb2, _ = alloc("ttb2", [128, QT], F32)
        gwb, _ = alloc("gwb", [128, NDC * NE], BF16)

        ctx.barrier()
        def ld_gw(g):
            src = d_gwT.rearrange("(c p) m -> p c m", p=128)
            dst = gwb[:, :].rearrange("p (c m) -> p c m", c=NDC)
            return g.dma_start(out=dst, in_=src)
        gw_ld = E("g", ld_gw, inc="sInit", incby=16)

        wgu_use = [0, 0, 0, 0]
        wdt_use = [0, 0]
        outst_free = [0, 0]

        for qg in range(NQG):
            if qg > 0:
                ctx.barrier()
            x1g_ld = 0
            for n_, v_ in x1_total.items():
                W("g", n_, v_)
            for i in range(4):
                def ldx1(g, i=i, qg=qg):
                    src = d_x1scr[i * 512:(i + 1) * 512, qg * QG:(qg + 1) * QG]
                    src = src.rearrange("(c p) t -> p c t", p=128)
                    dst = x1qg[:, i * 4 * QG:(i + 1) * 4 * QG].rearrange("p (c t) -> p c t", c=4)
                    return g.dma_start(out=dst, in_=src)
                x1g_ld = E("g", ldx1, inc="sXb", incby=16)

            preload(SILU_ID)
            W("t", "sInit", gw_ld)
            W("t", "sXb", x1g_ld)
            WB("t", ["A0", "A1", "A2", "A3"])
            glog = 0
            for e in range(NE):
                for dc in range(NDC):
                    for nt in range(2):
                        def mmg(t, e=e, dc=dc, nt=nt):
                            return t.matmul(pA[0:1, e * 1024 + nt * 512:e * 1024 + (nt + 1) * 512],
                                            gwb[:, dc * NE + e:dc * NE + e + 1],
                                            x1qg[:, dc * QG + nt * 512:dc * QG + (nt + 1) * 512],
                                            start=(dc == 0), stop=(dc == NDC - 1))
                        r = E("t", mmg, inc=("sT" if dc == NDC - 1 and nt == 1 else None))
                        if r is not None:
                            glog = r
            for hq in range(2):
                W("a", "sT", glog)
                l1c = E("a", lambda a, hq=hq: a.copy(
                    row_d[:, :], pA[0:1, 1024 + hq * 512:1024 + (hq + 1) * 512]), inc="sA")
                SG(["A%d" % (2 + hq)], "sA", l1c)
                W("v", "sA", l1c)
                df = E("v", lambda v, hq=hq: v.tensor_sub(
                    row_a[:, :], pA[0:1, hq * 512:(hq + 1) * 512], row_d[:, :]), inc="sV")
                SG(["A%d" % hq], "sV", df)
                W("a", "sV", df)
                tn = E("a", lambda a: a.activation(row_b[:, :], row_a[:, :], AF.Tanh,
                                                   scale=c_mhalf[0:1, 0:1]), inc="sA")
                W("v", "sA", tn)
                E("v", lambda v: v.tensor_scalar(row_b[:, :], row_b[:, :], -0.5, 0.5,
                                                 OP.mult, OP.add))
                E("v", lambda v: v.scalar_tensor_tensor(row_c[:, :], row_a[:, :], 0.0,
                                                        row_b[:, :], OP.is_ge, OP.mult))
                E("v", lambda v: v.tensor_scalar(row_b[:, :], row_b[:, :], -1.0, 1.0,
                                                 OP.mult, OP.add))
                cr = E("v", lambda v: v.scalar_tensor_tensor(row_d[:, :], row_a[:, :], 0.0,
                                                             row_b[:, :], OP.is_lt, OP.mult),
                       inc="sV")
                W("a", "sV", cr)
                E("a", lambda a: a.copy(row_bf[:, :], row_c[:, :]))
                cpc = E("a", lambda a: a.copy(row_bf2[:, :], row_d[:, :]), inc="sA")
                W("t", "sA", cpc)
                WB("t", ["B2", "B3"])
                E("t", lambda t: t.matmul(pB[:, 1024:1536], ones1[0:1, :], row_bf[0:1, :],
                                          start=True, stop=True))
                bc = E("t", lambda t: t.matmul(pB[:, 1536:2048], ones1[0:1, :], row_bf2[0:1, :],
                                               start=True, stop=True), inc="sT")
                W("a", "sT", bc)
                E("a", lambda a, hq=hq: a.copy(cbb[0][:, hq * 512:(hq + 1) * 512], pB[:, 1024:1536]))
                cbd = E("a", lambda a, hq=hq: a.copy(cbb[1][:, hq * 512:(hq + 1) * 512],
                                                     pB[:, 1536:2048]), inc="sA")
                SG(["B2", "B3"], "sA", cbd)
            gate_done = ctx.cnt["sA"]

            h_written = 0
            down_done = 0
            first_y = (True,)
            for e in range(NE):
                for fg in range(FGN):
                    for fs in range(NDC):
                        sl = fs % 2
                        fcol = fg * NDC + fs
                        W("g", "sT", wgu_use[sl])
                        def ldg(g, e=e, fcol=fcol, sl=sl):
                            src = d_wgT[e, :, fcol * 128:(fcol + 1) * 128]
                            src = src.rearrange("(c p) m -> p c m", p=128)
                            dst = wgu[sl][:, :].rearrange("p (c m) -> p c m", c=NDC)
                            return g.dma_start(out=dst, in_=src)
                        ldg_c = E("g", ldg, inc="sWgu%d" % sl, incby=16)
                        W("g", "sT", wgu_use[2 + sl])
                        def ldu(g, e=e, fcol=fcol, sl=sl):
                            src = d_wuT[e, :, fcol * 128:(fcol + 1) * 128]
                            src = src.rearrange("(c p) m -> p c m", p=128)
                            dst = wgu[2 + sl][:, :].rearrange("p (c m) -> p c m", c=NDC)
                            return g.dma_start(out=dst, in_=src)
                        wu_ld = E("g", ldu, inc="sWgu%d" % (2 + sl), incby=16)

                        psg = pA if sl == 0 else pB
                        gb = [("A" if sl == 0 else "B") + str(i) for i in range(2)]
                        ub = [("A" if sl == 0 else "B") + str(i) for i in range(2, 4)]
                        W("t", "sWgu%d" % sl, ldg_c)
                        W("t", "sWgu%d" % (2 + sl), wu_ld)
                        if e == 0 and fg == 0 and fs == 0:
                            W("t", "sA", gate_done)
                            W("v", "sA", gate_done)
                        WB("t", gb)
                        for dc in range(NDC):
                            for nt in range(2):
                                def mmG(t, psg=psg, sl=sl, dc=dc, nt=nt):
                                    return t.matmul(psg[:, nt * 512:(nt + 1) * 512],
                                                    wgu[sl][:, dc * 128:(dc + 1) * 128],
                                                    x1qg[:, dc * QG + nt * 512:dc * QG + (nt + 1) * 512],
                                                    start=(dc == 0), stop=(dc == NDC - 1))
                                g_mm = E("t", mmG,
                                         inc=("sT" if dc == NDC - 1 and nt == 1 else None))
                                if g_mm is not None:
                                    gmm = g_mm
                        WB("t", ub)
                        for dc in range(NDC):
                            for nt in range(2):
                                def mmU(t, psg=psg, sl=sl, dc=dc, nt=nt):
                                    return t.matmul(psg[:, 1024 + nt * 512:1024 + (nt + 1) * 512],
                                                    wgu[2 + sl][:, dc * 128:(dc + 1) * 128],
                                                    x1qg[:, dc * QG + nt * 512:dc * QG + (nt + 1) * 512],
                                                    start=(dc == 0), stop=(dc == NDC - 1))
                                u_mm = E("t", mmU,
                                         inc=("sT" if dc == NDC - 1 and nt == 1 else None))
                                if u_mm is not None:
                                    umm = u_mm
                        wgu_use[sl] = gmm
                        wgu_use[2 + sl] = umm
                        W("a", "sT", gmm)
                        if fs >= 2 or e + fg > 0:
                            W("a", "sV", h_written)
                        sg_done = E("a", lambda a, psg=psg, sl=sl: a.activation(
                            sgt[sl][:, :], psg[:, 0:1024], AF.Silu), inc="sA")
                        SG(gb, "sA", sg_done)
                        W("v", "sA", sg_done)
                        W("v", "sT", umm)
                        if e + fg > 0:
                            W("v", "sT", down_done)
                        def hmul(v, psg=psg, sl=sl, fs=fs):
                            return v.scalar_tensor_tensor(hsb[:, fs * QG:(fs + 1) * QG],
                                                          psg[:, 1024:2048], c_onef[:, 0:1],
                                                          sgt[sl][:, :], OP.mult, OP.mult)
                        h_written = E("v", hmul, inc="sV")
                        SG(ub, "sV", h_written)
                    h_all = h_written
                    for ds in range(NDC):
                        sl = ds % 2
                        W("g", "sT", wdt_use[sl])
                        def ldd(g, e=e, fg=fg, ds=ds, sl=sl):
                            src = d_wdT[e, fg * 2048:(fg + 1) * 2048, ds * 128:(ds + 1) * 128]
                            src = src.rearrange("(c p) m -> p c m", p=128)
                            dst = wdt[sl][:, :].rearrange("p (c m) -> p c m", c=NDC)
                            return g.dma_start(out=dst, in_=src)
                        wd_ld = E("g", ldd, inc="sWdt%d" % sl, incby=16)
                        psd = pB[:, sl * 1024:(sl + 1) * 1024]
                        dbk = ["B%d" % (2 * sl), "B%d" % (2 * sl + 1)]
                        W("t", "sWdt%d" % sl, wd_ld)
                        W("t", "sV", h_all)
                        WB("t", dbk)
                        for fc in range(NDC):
                            for nt in range(2):
                                def mmD(t, psd=psd, sl=sl, fc=fc, nt=nt):
                                    return t.matmul(psd[:, nt * 512:(nt + 1) * 512],
                                                    wdt[sl][:, fc * 128:(fc + 1) * 128],
                                                    hsb[:, fc * QG + nt * 512:fc * QG + (nt + 1) * 512],
                                                    start=(fc == 0), stop=(fc == NDC - 1))
                                E("t", mmD, inc=("sT" if fc == NDC - 1 and nt == 1 else None))
                        wdt_use[sl] = ctx.cnt["sT"]
                        W("v", "sT", ctx.cnt["sT"])
                        if first_y[0]:
                            def yinit(v, psd=psd, ds=ds, e=e):
                                return v.scalar_tensor_tensor(yacc[:, ds * QG:(ds + 1) * QG],
                                                              psd, c_onef[:, 0:1], cbb[e][:, :],
                                                              OP.mult, OP.mult)
                            yd = E("v", yinit, inc="sV")
                        else:
                            def ymul(v, psd=psd, sl=sl, e=e):
                                return v.scalar_tensor_tensor(ytmp[sl][:, :], psd, c_onef[:, 0:1],
                                                              cbb[e][:, :], OP.mult, OP.mult)
                            E("v", ymul)
                            def yadd(v, ds=ds, sl=sl):
                                return v.tensor_add(yacc[:, ds * QG:(ds + 1) * QG],
                                                    yacc[:, ds * QG:(ds + 1) * QG], ytmp[sl][:, :])
                            yd = E("v", yadd, inc="sV")
                        SG(dbk, "sV", yd)
                        down_done = yd
                    first_y = (False,)

            preload(LNEXP_ID)
            y_all = ctx.cnt["sV"]
            if debug and qg == 0:
                W("y", "sV", y_all)
                for dsc in range(NDC):
                    E("y", lambda y, dsc=dsc: y.dma_start(
                        out=d_dby[dsc * 128:(dsc + 1) * 128, :],
                        in_=yacc[:, dsc * QG:(dsc + 1) * QG]), inc="sDbg", incby=16)
                for e_ in range(NE):
                    E("y", lambda y, e_=e_: y.dma_start(out=d_dbc[e_], in_=cbb[e_][:, :]),
                      inc="sDbg", incby=16)
                W("y", "sDbg", ctx.cnt["sDbg"])
                W("v", "sDbg", ctx.cnt["sDbg"])
                W("a", "sDbg", ctx.cnt["sDbg"])
            for qtl in range(2):
                W("a", "sV", y_all)
                res_cnt = [0] * NDC
                for dsc in range(NDC):
                    sl = dsc % 2
                    if dsc >= 2:
                        W("a", "sV", res_cnt[dsc - 2])
                    W("a", "sOut%d" % sl, outst_free[sl])
                    cx = E("a", lambda a, dsc=dsc, qtl=qtl, sl=sl: a.copy(
                        outst[sl][:, :],
                        x1qg[:, dsc * QG + qtl * 512:dsc * QG + (qtl + 1) * 512]), inc="sA")
                    W("v", "sA", cx)
                    def resadd(v, dsc=dsc, qtl=qtl, sl=sl):
                        off = dsc * QG + qtl * 512
                        return v.tensor_add(yacc[:, off:off + 512], yacc[:, off:off + 512],
                                            outst[sl][:, :])
                    res_cnt[dsc] = E("v", resadd, inc="sV")
                res_done = ctx.cnt["sV"]
                W("a", "sV", res_done)
                st_cnt = [0] * NDC
                for dsc in range(NDC):
                    sl = dsc % 2
                    if dsc >= 2:
                        W("a", "sT", st_cnt[dsc - 2])
                    off = dsc * QG + qtl * 512
                    E("a", lambda a, off=off, sl=sl: a.copy(stb[sl][:, :], yacc[:, off:off + 512]))
                    sq = E("a", lambda a, off=off, sl=sl: a.activation(
                        sqb[sl][:, :], yacc[:, off:off + 512], AF.Square), inc="sA")
                    W("t", "sA", sq)
                    if dsc == 0:
                        WB("t", ["A0", "A1"])
                    E("t", lambda t, sl=sl, dsc=dsc: t.matmul(pA[0:1, 0:512], onesk[:, 0:1],
                                                              stb[sl][:, :], start=(dsc == 0),
                                                              stop=(dsc == NDC - 1)))
                    st_cnt[dsc] = E("t", lambda t, sl=sl, dsc=dsc: t.matmul(
                        pA[0:1, 512:1024], onesk[:, 0:1], sqb[sl][:, :],
                        start=(dsc == 0), stop=(dsc == NDC - 1)), inc="sT")
                W("v", "sT", st_cnt[NDC - 1])
                bb_done = ln_rows_and_bcast()
                W("v", "sA", bb_done)
                for dsc in range(NDC):
                    sl = dsc % 2
                    off = dsc * QG + qtl * 512
                    W("v", "sOut%d" % sl, outst_free[sl])
                    E("v", lambda v, off=off: v.tensor_mul(ttb2[:, :], yacc[:, off:off + 512],
                                                           bb0[:, :]))
                    nv = E("v", lambda v, sl=sl: v.tensor_sub(outst[sl][:, :], ttb2[:, :],
                                                              bb1[:, :]), inc="sV")
                    W("y", "sV", nv)
                    def stout(y, dsc=dsc, qg=qg, qtl=qtl, sl=sl):
                        col = qg * QG + qtl * 512
                        return y.dma_start(out=d_out[dsc * 128:(dsc + 1) * 128, col:col + 512],
                                           in_=outst[sl][:, :])
                    outst_free[sl] = E("y", stout, inc="sOut%d" % sl, incby=16)

        W("y", "sOut0", ctx.cnt["sOut0"])
        W("y", "sOut1", ctx.cnt["sOut1"])
        ctx.flush()
        nc._dbg_cnt = dict(ctx.cnt)

        for cm in reversed(ctx.sem_cms):
            cm.__exit__(None, None, None)

    for cm in reversed(open_cms):
        cm.__exit__(None, None, None)
    return nc



def _own_token_ranges(half):
    return [(qi * QT, (qi + 1) * QT) for qi in SLOT_SETS[half]]


def _thr_host(half):
    thr = np.full((128, 128), 1000.0, dtype=np.float32)
    p = np.arange(128, dtype=np.float32)
    for j, qi in enumerate(SLOT_SETS[half]):
        g = qi * QT
        for kb in range(NMAX[j]):
            if kb < NREAL[half][j]:
                thr[:, j * 32 + kb] = 128.0 * kb + p - g
    return thr


_NC_CACHE = {}


def kernel(L0_attn_x, L0_attn_W_qkv, L0_attn_W_proj, L0_ffn_gate_w,
           L0_ffn_w_gate, L0_ffn_w_up, L0_ffn_w_down):
    x = np.asarray(L0_attn_x, dtype=np.float32)
    Wqkv = np.asarray(L0_attn_W_qkv, dtype=np.float32)
    Wproj = np.asarray(L0_attn_W_proj, dtype=np.float32)
    gate_w = np.asarray(L0_ffn_gate_w, dtype=np.float32)
    w_gate = np.asarray(L0_ffn_w_gate, dtype=np.float32)
    w_up = np.asarray(L0_ffn_w_up, dtype=np.float32)
    w_down = np.asarray(L0_ffn_w_down, dtype=np.float32)

    wqT = np.ascontiguousarray(Wqkv[0:D].T)
    wkT = np.ascontiguousarray(Wqkv[D:2 * D].T)
    wvT = np.ascontiguousarray(Wqkv[2 * D:3 * D].T)
    wpT = np.ascontiguousarray(Wproj.T)
    gwT = np.ascontiguousarray(gate_w.T)
    wgT = np.ascontiguousarray(np.transpose(w_gate, (0, 2, 1)))
    wuT = np.ascontiguousarray(np.transpose(w_up, (0, 2, 1)))
    wdT = np.ascontiguousarray(np.transpose(w_down, (0, 2, 1)))
    iota = np.ascontiguousarray(np.broadcast_to(np.arange(QT, dtype=np.float32), (128, QT)))
    thr = [_thr_host(0), _thr_host(1)]
    xT = [np.ascontiguousarray(x[b].T) for b in range(B)]

    in_maps = []
    for c in range(NC := 8):
        b, half = c // 2, c % 2
        xo = np.ascontiguousarray(
            np.concatenate([x[b][r0:r1] for (r0, r1) in _own_token_ranges(half)], axis=0).T)
        in_maps.append({
            "xT": xT[b], "xown": xo, "thr": thr[half], "iota": iota,
            "wqT": wqT, "wkT": wkT, "wvT": wvT, "wpT": wpT, "gwT": gwT,
            "wgT": wgT, "wuT": wuT, "wdT": wdT,
        })

    if "nc" not in _NC_CACHE:
        _NC_CACHE["nc"] = build_nc()
    res = bass_utils.run_bass_kernel_spmd(_NC_CACHE["nc"], in_maps, core_ids=list(range(8)))

    out = np.empty((B, S, D), dtype=np.float32)
    for c in range(8):
        b, half = c // 2, c % 2
        oT = np.asarray(res.results[c]["out"]).T
        for j, (r0, r1) in enumerate(_own_token_ranges(half)):
            out[b, r0:r1] = oT[j * QT:(j + 1) * QT]
    return out
